# version 10
# speedup vs baseline: 1.5992x; 1.2396x over previous
import math
from contextlib import ExitStack

import numpy as np
import ml_dtypes
import concourse.bass as bass
import concourse.mybir as mybir
from concourse.bass_utils import run_bass_kernel_spmd


def recip_approx(vector, out, in_, scr):
    ins = vector.reciprocal(out, in_)
    vector.drain()
    return ins

F32 = mybir.dt.float32
BF16 = mybir.dt.bfloat16
AF = mybir.ActivationFunctionType
ALU = mybir.AluOpType
AX = mybir.AxisListType
BF16NP = ml_dtypes.bfloat16

B, T, D = 4, 4096, 1024
HALF = T // 2
P = 128
NEG_BIG = -1.0e30


class Sem:
    def __init__(self, es, nc, name):
        self.h = es.enter_context(nc.semaphore(name))
        self.n = 0

    def inc(self, instr, k=1):
        instr.then_inc(self.h, k)
        self.n += k
        return instr

    def wait(self, eng, n=None):
        eng.wait_ge(self.h, self.n if n is None else n)


class Ctx:
    def __init__(self, nc):
        self.nc = nc
        self.es = ExitStack()

    def sb(self, name, shape, dtype):
        return self.es.enter_context(self.nc.sbuf_tensor(name, shape, dtype))

    def ps(self, name, shape, dtype=F32):
        return self.es.enter_context(self.nc.psum_tensor(name, shape, dtype))

    def sem(self, name):
        return Sem(self.es, self.nc, name)


class DmaStream:

    def __init__(self, cx, name, nbuf=2):
        self.nbuf = nbuf
        self.sems = [cx.sem(f"{name}{p}") for p in range(nbuf)]
        self.base = [s.n for s in self.sems]

    def issue(self, eng, i, out, in_):
        p = i % self.nbuf
        eng.dma_start(out=out, in_=in_).then_inc(self.sems[p].h, 16)
        self.sems[p].n += 16

    def wait(self, eng, i):
        p = i % self.nbuf
        eng.wait_ge(self.sems[p].h, self.base[p] + 16 * (i // self.nbuf + 1))

ENGS = ("tensor", "vector", "scalar", "gpsimd", "sync")


class Shared:

    def __init__(self, nc, block, es):
        self.nc = nc
        self.block = block
        self.em = Emitter(block)
        self.es = es
        self.pool = {}

    def sem(self, name):
        if name not in self.pool:
            self.pool[name] = Sem(self.es, self.nc, name)
        return self.pool[name]

    def barrier(self):
        sems = [s for s in self.pool.values() if s.n]
        for eng_name in ENGS:
            def bar(eng, sems=tuple(sems)):
                for s in sems:
                    s.wait(eng)
            self.em.on(eng_name, bar)


class StageCtx:

    _uid = [0]

    def __init__(self, S):
        self.nc = S.nc
        self.S = S
        self.es = ExitStack()

    def sb(self, name, shape, dtype):
        StageCtx._uid[0] += 1
        return self.es.enter_context(
            self.nc.sbuf_tensor(f"{name}_{StageCtx._uid[0]}", shape, dtype))

    def ps(self, name, shape, dtype=F32):
        StageCtx._uid[0] += 1
        return self.es.enter_context(
            self.nc.psum_tensor(f"{name}_{StageCtx._uid[0]}", shape, dtype))

    def sem(self, name):
        return self.S.sem(name)


class Emitter:

    def __init__(self, block):
        self.block = block

    def on(self, eng_name, fn):
        getattr(self.block, eng_name)(fn)




def np_consts():
    c = {}
    c["cc_ident"] = np.eye(P, dtype=BF16NP)
    return c


def emit_stage_a(S, io, T=T):

    nc = S.nc
    block = S.block
    x = io["a_x"]
    ident_d = io["cc_ident"]
    xn_fm_d = io["a_xn_fm"]
    cx = StageCtx(S)
    NT = T // P
    xt = [cx.sb(f"xt{p}", [P, D], F32) for p in range(2)]
    xsq = cx.sb("xsq", [P, D], F32)
    st = cx.sb("st", [P, 16], F32)
    xnb = [cx.sb(f"xnb{p}", [P, D], BF16) for p in range(2)]
    ident = cx.sb("ident", [P, P], BF16)
    fm = cx.sb("fm", [P, 8 * T], BF16)
    fm_r = fm.ap().rearrange("p (c t) -> p c t", c=8)
    ptr = [cx.ps(f"ptr{p}", [P, 512], BF16) for p in range(2)]
    ptr_r = [t.ap().rearrange("p (c t) -> p c t", c=4) for t in ptr]

    ldx = DmaStream(cx, "ldx", 2)
    ldi = cx.sem("ldi")
    ldi_b = ldi.n
    s_sq = cx.sem("s_sq")
    s_st = cx.sem("s_st")
    s_rs = cx.sem("s_rs")
    s_ap = cx.sem("s_ap")
    s_tr = cx.sem("s_tr")
    s_ev = cx.sem("s_ev")
    s_out = cx.sem("s_out")
    sq_b, st_b, rs_b, ap_b, tr_b, ev_b, out_b = (s_sq.n, s_st.n, s_rs.n, s_ap.n,
                                                 s_tr.n, s_ev.n, s_out.n)

    with cx.es:
        @block.sync
        def _(sync):
            sync.dma_start(out=ident[:], in_=ident_d[:, :]).then_inc(ldi.h, 16)
            ldi.n += 16
            for i in range(NT):
                if i >= 2:
                    sync.wait_ge(s_ap.h, ap_b + i - 1)
                ldx.issue(sync, i, xt[i % 2][:], x[P * i:P * (i + 1), :])

        @block.scalar
        def _(scalar):
            for i in range(NT):
                o = 8 * (i % 2)
                scalar.wait_ge(s_st.h, st_b + i + 1)
                s_rs.inc(scalar.activation(out=st[:, o + 4:o + 5], in_=st[:, o + 3:o + 4],
                                           func=AF.Sqrt))

        @block.vector
        def _(vector):
            for i in range(NT):
                o = 8 * (i % 2)
                b = xt[i % 2]
                ldx.wait(vector, i)
                if i >= 2:
                    vector.wait_ge(s_rs.h, rs_b + i - 1)
                vector.tensor_mul(xsq[:], b[:], b[:])
                vector.drain()
                vector.reduce_sum(st[:, o + 1:o + 2], xsq[:], axis=AX.X)
                vector.drain()
                vector.reduce_sum(st[:, o + 0:o + 1], b[:], axis=AX.X)
                vector.drain()
                vector.tensor_scalar_mul(st[:, o + 2:o + 3], st[:, o + 0:o + 1], -1.0 / D)
                vector.drain()
                vector.tensor_mul(st[:, o + 3:o + 4], st[:, o + 2:o + 3], st[:, o + 2:o + 3])
                vector.drain()
                vector.scalar_tensor_tensor(st[:, o + 3:o + 4], st[:, o + 1:o + 2], 1.0 / D,
                                            st[:, o + 3:o + 4], ALU.mult, ALU.subtract)
                vector.drain()
                s_st.inc(vector.tensor_scalar_add(st[:, o + 3:o + 4], st[:, o + 3:o + 4], 1e-5))
                vector.wait_ge(s_rs.h, rs_b + i + 1)
                recip_approx(vector, st[:, o + 5:o + 6], st[:, o + 4:o + 5],
                             st[:, o + 6:o + 7])
                if i >= 2:
                    vector.wait_ge(s_tr.h, tr_b + 8 * (i - 1))
                s_ap.inc(vector.tensor_scalar(xnb[i % 2][:], b[:], st[:, o + 2:o + 3],
                                              st[:, o + 5:o + 6], ALU.add, ALU.mult))
                if i > 0:
                    j = i - 1
                    for g in range(2):
                        vector.wait_ge(s_tr.h, tr_b + 8 * j + 4 * (g + 1))
                        s_ev.inc(vector.tensor_copy(
                            fm_r[:, 4 * g:4 * g + 4, P * j:P * (j + 1)],
                            ptr_r[g][:, :, :]))
            j = NT - 1
            for g in range(2):
                vector.wait_ge(s_tr.h, tr_b + 8 * j + 4 * (g + 1))
                s_ev.inc(vector.tensor_copy(fm_r[:, 4 * g:4 * g + 4, P * j:P * (j + 1)],
                                            ptr_r[g][:, :, :]))

        @block.tensor
        def _(tensor):
            tensor.wait_ge(ldi.h, ldi_b + 16)
            for i in range(NT):
                tensor.wait_ge(s_ap.h, ap_b + i + 1)
                if i >= 1:
                    tensor.wait_ge(s_ev.h, ev_b + 2 * (i - 1) + 1)
                for c in range(4):
                    s_tr.inc(tensor.transpose(ptr[0][:, P * c:P * (c + 1)],
                                              xnb[i % 2][:, P * c:P * (c + 1)], ident[:]))
                if i >= 1:
                    tensor.wait_ge(s_ev.h, ev_b + 2 * (i - 1) + 2)
                for c in range(4):
                    s_tr.inc(tensor.transpose(ptr[1][:, P * c:P * (c + 1)],
                                              xnb[i % 2][:, P * (c + 4):P * (c + 5)], ident[:]))

        @block.gpsimd
        def _(gpsimd):
            gpsimd.wait_ge(s_ev.h, ev_b + 2 * NT)
            for c in range(8):
                gpsimd.dma_start(out=xn_fm_d[P * c:P * (c + 1), :],
                                 in_=fm[:, T * c:T * (c + 1)]).then_inc(s_out.h, 16)
                s_out.n += 16
            gpsimd.wait_ge(s_out.h, out_b + 16 * 8)

    S.barrier()




def io_stage_a(nc, T=T, pref="", ki="ExternalInput", ko="ExternalOutput"):
    io = {}
    io["a_x"] = nc.dram_tensor(pref + "a_x", [T, D], F32, kind=ki).ap()
    io["cc_ident"] = nc.dram_tensor("cc_ident", [P, P], BF16, kind=ki).ap()
    io["a_xn_fm"] = nc.dram_tensor(pref + "a_xn_fm", [D, T], BF16, kind=ko).ap()
    return io


def build_stage_a(T=T):
    nc = bass.Bass("TRN2", target_bir_lowering=False, debug=False)
    io = io_stage_a(nc, T=T)
    es = ExitStack()
    with es, nc.Block() as block:
        S = Shared(nc, block, es)
        emit_stage_a(S, io, T=T)
    return nc


def attn_consts():
    c = {}
    s = np.arange(P)[:, None]
    t = np.arange(512)[None, :]
    m = np.concatenate([(t >= 128 * p + s).astype(np.float32) for p in range(4)], axis=1)
    c["cc_mask"] = m.astype(BF16NP)
    cols = np.zeros((P, 4), np.float32)
    cols[:, 0] = 1.0
    cols[0:64, 1] = 1.0
    cols[64:128, 2] = 1.0
    c["cc_cols_bf"] = cols.astype(BF16NP)
    c["cc_cols_f32"] = cols
    rowsel = np.zeros((1, 768), np.float32)
    rowsel[0, 0:512] = 1.0
    rowsel[0, 512:576] = 1.0
    rowsel[0, 640 + 64:640 + 128] = 1.0
    c["cc_row_f32"] = rowsel
    c["cc_row_bf"] = rowsel.astype(BF16NP)
    return c


def emit_stage_b(S, io, T=T):

    nc = S.nc
    block = S.block
    xn_d = io["b_xn_fm"]
    w_d = {n: io[f"b_w{n}"] for n in "qkv"}
    b_d = {n: io[f"b_b{n}"] for n in "qkv"}
    mask_d = io["cc_mask"]
    colsb_d = io["cc_cols_bf"]
    rowf_d = io["cc_row_f32"]
    y_d = io["b_y"]
    cx = StageCtx(S)
    NTB = T // 512
    NKC = T // P
    xn = cx.sb("xn", [P, 8 * T], BF16)
    ws = {n: cx.sb(f"w{n}s", [P, 8 * 512], BF16) for n in "qkv"}
    bcol = {n: cx.sb(f"b{n}c", [P, 4], F32) for n in "qk"}
    bvrow = cx.sb("bvrow", [1, 512], F32)
    mask = cx.sb("mask", [P, 2048], BF16)
    colsb = cx.sb("colsb", [P, 4], BF16)
    rowf = cx.sb("rowf", [1, 768], F32)
    vbias = cx.sb("vbias", [P, 512], F32)
    v_sb = cx.sb("v_sb", [P, NKC * 512], BF16)
    q_sb = cx.sb("q_sb", [P, T], BF16)
    k_sb = cx.sb("k_sb", [P, T], BF16)
    y_sb = cx.sb("y_sb", [P, T], BF16)
    sqs = cx.sb("sqs", [P, 512], BF16)
    mx = cx.sb("mx", [1, 64], F32)
    pt = [cx.sb(f"pt{i}", [P, 1024], BF16) for i in range(3)]
    rsum = cx.sb("rsum", [1, 1024], F32)
    ysc = cx.sb("ysc", [P, 512], F32)
    rr = cx.sb("rr", [P, 512], F32)
    bias16 = cx.sb("bias16", [P, 16], F32)
    psA = [cx.ps(f"psA{i}", [P, 512]) for i in range(3)]
    psB = [cx.ps(f"psB{i}", [P, 512]) for i in range(3)]
    psY = cx.ps("psY", [P, 512])
    psS = cx.ps("psS", [P, 512])
    psM = psB[2]
    psQ = psA[2]

    ldw = cx.sem("ldw")
    ldc = cx.sem("ldc")
    ldx = cx.sem("ldx")
    s_mm = cx.sem("s_mm")
    s_ea = cx.sem("s_ea")
    s_ev = cx.sem("s_ev")
    s_sc = cx.sem("s_sc")
    s_av = cx.sem("s_av")
    s_out = cx.sem("s_out")
    SEMS = [ldw, ldc, ldx, s_mm, s_ea, s_ev, s_sc, s_av, s_out]

    def barrier(eng):
        for s in SEMS:
            if s.n:
                s.wait(eng)

    with cx.es:
        em = S.em

        def p0_sync(sync):
            for c in range(8):
                sync.dma_start(out=xn[:, T * c:T * (c + 1)],
                               in_=xn_d[P * c:P * (c + 1), :]).then_inc(ldx.h, 16)
            ldx.n += 8 * 16
            for dst, src in ((mask, mask_d), (colsb, colsb_d), (rowf, rowf_d)):
                sync.dma_start(out=dst[:], in_=src[:, :]).then_inc(ldc.h, 16)
            ldc.n += 3 * 16
        em.on("sync", p0_sync)

        def p0_gp(gpsimd):
            for n in "qkv":
                for kc in range(8):
                    gpsimd.dma_start(out=ws[n][:, 512 * kc:512 * (kc + 1)],
                                     in_=w_d[n][P * kc:P * (kc + 1), :]).then_inc(ldw.h, 16)
                ldw.n += 8 * 16
            for n in "qk":
                for mt in range(4):
                    gpsimd.dma_start(out=bcol[n][:, mt:mt + 1],
                                     in_=b_d[n][0:1, 128 * mt:128 * (mt + 1)].rearrange("o p -> p o")
                                     ).then_inc(ldw.h, 16)
                ldw.n += 4 * 16
            gpsimd.dma_start(out=bvrow[:], in_=b_d["v"][0:1, :]).then_inc(ldw.h, 16)
            ldw.n += 16
        em.on("gpsimd", p0_gp)

        def p0_ve(vector):
            ldw.wait(vector)
            ldc.wait(vector)
            s_ev.inc(vector.tensor_scalar_mul(bcol["q"][:], bcol["q"][:], 0.125))
        em.on("vector", p0_ve)

        def p0_pe(tensor):
            ldw.wait(tensor)
            ldc.wait(tensor)
            s_mm.inc(tensor.matmul(psM[:, 0:512], rowf[0:1, 0:128], bvrow[0:1, :],
                                   start=True, stop=True))
        em.on("tensor", p0_pe)

        def p0_vb(vector):
            s_mm.wait(vector)
            s_ea.inc(vector.tensor_copy(vbias[:], psM[:, 0:512]))
        em.on("vector", p0_vb)

        vb_mark = s_ea.n
        for tt in range(NKC):
            def p1_pe(tensor, tt=tt):
                if tt == 0:
                    ldx.wait(tensor)
                    ldw.wait(tensor)
                    ldc.wait(tensor)
                    tensor.wait_ge(s_ea.h, vb_mark)
                if tt >= 2:
                    s_ev.wait(tensor, s_ev.n - 1)
                ps = psA[tt % 2]
                for kc in range(8):
                    ins = tensor.matmul(ps[:], xn[:, T * kc + P * tt:T * kc + P * (tt + 1)],
                                        ws["v"][:, 512 * kc:512 * (kc + 1)],
                                        start=(kc == 0), stop=(kc == 7))
                s_mm.inc(ins)
            em.on("tensor", p1_pe)

            def p1_ve(vector, tt=tt):
                s_mm.wait(vector)
                s_ev.inc(vector.tensor_add(v_sb[:, 512 * tt:512 * (tt + 1)],
                                           psA[tt % 2][:], vbias[:]))
            em.on("vector", p1_ve)

        for hp in range(4):
            for which, wname in ((0, "q"), (1, "k")):
                dst = q_sb if which == 0 else k_sb
                pss = psA if which == 0 else psB
                for tb in range(NTB):
                    def p2_pe(tensor, which=which, wname=wname, tb=tb, pss=pss, hp=hp):
                        if hp == 0 and which == 0 and tb == 0:
                            barrier(tensor)
                        if hp > 0 and which == 0 and tb == 0:
                            s_sc.wait(tensor)
                        if tb >= 2:
                            s_ea.wait(tensor, s_ea.n - 1)
                        ps = pss[tb % 2]
                        for kc in range(8):
                            ins = tensor.matmul(ps[:],
                                                ws[wname][:, 512 * kc + 128 * hp:512 * kc + 128 * (hp + 1)],
                                                xn[:, T * kc + 512 * tb:T * kc + 512 * (tb + 1)],
                                                start=(kc == 0), stop=(kc == 7))
                        s_mm.inc(ins)
                    em.on("tensor", p2_pe)

                    def p2_ve(vector, which=which, wname=wname, tb=tb, dst=dst, pss=pss, hp=hp):
                        s_mm.wait(vector)
                        if hp > 0 and which == 0 and tb == 0:
                            s_out.wait(vector)
                        s_ea.inc(vector.tensor_scalar(
                            dst[:, 512 * tb:512 * (tb + 1)], pss[tb % 2][:],
                            0.125 if which == 0 else 1.0,
                            bcol[wname][:, hp:hp + 1], ALU.mult, ALU.add))
                    em.on("vector", p2_ve)

            for src_i, src in ((0, q_sb), (1, k_sb)):
                for tb in range(NTB):
                    sl = src[:, 512 * tb:512 * (tb + 1)]

                    def p3_sq(vector, sl=sl):
                        s_mm.wait(vector)
                        vector.drain()
                        s_ev.inc(vector.tensor_mul(sqs[:], sl, sl))
                    em.on("vector", p3_sq)
                    sq_mark = s_ev.n

                    def p3_pe(tensor, sq_mark=sq_mark):
                        tensor.wait_ge(s_ev.h, sq_mark)
                        i1 = tensor.matmul(psQ[0:1, 0:512], colsb[:, 1:2], sqs[:],
                                           start=True, stop=True, tile_position=(0, 0))
                        i2 = tensor.matmul(psQ[32:33, 0:512], colsb[:, 2:3], sqs[:],
                                           start=True, stop=True, tile_position=(0, 32),
                                           skip_group_check=True)
                        s_mm.inc(i1)
                        s_mm.inc(i2)
                    em.on("tensor", p3_pe)
                    mm_mark = s_mm.n

                    def p3_max(vector, tb=tb, src_i=src_i, mm_mark=mm_mark):
                        vector.wait_ge(s_mm.h, mm_mark)
                        if src_i == 0:
                            vector.reduce_max(mx[0:1, tb:tb + 1], psQ[0:1, 0:512],
                                              axis=AX.X)
                            vector.drain()
                            vector.reduce_max(mx[0:1, 8 + tb:9 + tb], psQ[32:33, 0:512],
                                              axis=AX.X)
                            vector.drain()
                            s_ev.inc(vector.nop())
                        else:
                            vector.reduce_max(mx[0:1, 20:21], psQ[0:1, 0:512], axis=AX.X)
                            vector.drain()
                            vector.reduce_max(mx[0:1, 21:22], psQ[32:33, 0:512], axis=AX.X)
                            vector.drain()
                            if tb == 0:
                                ins = vector.tensor_copy(mx[0:1, 16:18], mx[0:1, 20:22])
                            else:
                                ins = vector.tensor_max(mx[0:1, 16:18], mx[0:1, 16:18],
                                                        mx[0:1, 20:22])
                            vector.drain()
                            s_ev.inc(ins)
                    em.on("vector", p3_max)

            def p3_prod(vector):
                for hh in range(2):
                    vector.tensor_scalar(mx.ap()[0:1, 32 + hh:32 + hh + 2 * NTB:2],
                                         mx[0:1, 8 * hh:8 * hh + NTB],
                                         mx[0:1, 16 + hh:17 + hh], 1.0,
                                         ALU.mult, ALU.mult)
                    vector.drain()
                s_ev.inc(vector.nop())
            em.on("vector", p3_prod)
            prod_mark = s_ev.n

            def p3_sqB(scalar, prod_mark=prod_mark):
                scalar.wait_ge(s_ev.h, prod_mark)
                s_ea.inc(scalar.activation(out=mx[0:1, 48:48 + 2 * NTB],
                                           in_=mx[0:1, 32:32 + 2 * NTB], func=AF.Sqrt))
            em.on("scalar", p3_sqB)
            sqB_mark = s_ea.n

            def p3_neg(vector, sqB_mark=sqB_mark):
                vector.wait_ge(s_ea.h, sqB_mark)
                vector.tensor_scalar_mul(mx[0:1, 32:32 + 2 * NTB],
                                         mx[0:1, 48:48 + 2 * NTB], -1.0)
                vector.drain()
                s_ev.inc(vector.nop())
            em.on("vector", p3_neg)

            def p4_b16(tensor):
                s_ev.wait(tensor)
                s_ea.wait(tensor)
                s_mm.inc(tensor.matmul(psM[:, 0:2 * NTB], rowf[0:1, 0:128],
                                       mx[0:1, 32:32 + 2 * NTB], start=True, stop=True))
            em.on("tensor", p4_b16)
            b16mm_mark = s_mm.n

            def p4_b16cp(vector, b16mm_mark=b16mm_mark):
                vector.wait_ge(s_mm.h, b16mm_mark)
                s_ea.inc(vector.tensor_copy(bias16[:, 0:2 * NTB], psM[:, 0:2 * NTB]))
            em.on("vector", p4_b16cp)
            bcp_hp = s_ea.n

            ea0, ev0, mm0, sc0, av0 = s_ea.n, s_ev.n, s_mm.n, s_sc.n, s_av.n
            out0 = s_out.n
            M = {}
            ea, ev, mm, sc, av = ea0, ev0, mm0, sc0, av0
            for qb in range(NTB):
                nkc = 4 * (qb + 1)
                for kc in range(nkc):
                    sc += 2; M["sc", qb, kc] = sc
                    ea += 1; M["exp", qb, kc] = ea
                    if kc >= 4 * qb:
                        ev += 1; M["msk", qb, kc] = ev
                    av += 4; M["av", qb, kc] = av
                M["lastav", qb] = av
                ea += 1; M["yev", qb] = ea
                ev += 1; M["rcp", qb] = ev
                mm += 1; M["rb", qb] = mm
                ev += 1; M["yn", qb] = ev

            def p4_tensor(tensor):
                def emit_sc(qb, kc):
                    tensor.wait_ge(s_ea.h, M["exp", qb, kc - 3] if kc >= 3
                                   else (M["yev", qb - 1] if qb else bcp_hp))
                    tensor.matmul(psA[kc % 3][:], k_sb[0:64, P * kc:P * (kc + 1)],
                                  q_sb[0:64, 512 * qb:512 * (qb + 1)],
                                  start=True, stop=True,
                                  tile_position=(0, 0)).then_inc(s_sc.h, 1)
                    tensor.matmul(psB[kc % 3][:], k_sb[64:128, P * kc:P * (kc + 1)],
                                  q_sb[64:128, 512 * qb:512 * (qb + 1)],
                                  start=True, stop=True,
                                  tile_position=(64, 0)).then_inc(s_sc.h, 1)

                def emit_av(qb, kc, nkc):
                    tensor.wait_ge(s_ea.h, M["exp", qb, kc])
                    if ("msk", qb, kc) in M:
                        tensor.wait_ge(s_ev.h, M["msk", qb, kc])
                    if kc == 0:
                        tensor.wait_ge(s_ea.h, M["yev", qb - 1] if qb else ea0)
                        tensor.wait_ge(s_ev.h, M["yn", qb - 1] if qb else ev0)
                    st_, sp_ = kc == 0, kc == nkc - 1
                    c0 = 512 * kc + 64 * (2 * hp)
                    c1 = 512 * kc + 64 * (2 * hp + 1)
                    for ii in (
                        tensor.matmul(psY[0:64, :], v_sb[:, c0:c0 + 64],
                                      pt[kc % 3][:, 0:512], start=st_, stop=sp_,
                                      tile_position=(0, 0)),
                        tensor.matmul(psY[64:128, :], v_sb[:, c1:c1 + 64],
                                      pt[kc % 3][:, 512:1024], start=st_, stop=sp_,
                                      tile_position=(0, 64), skip_group_check=True),
                        tensor.matmul(psS[0:1, 0:512], colsb[:, 0:1],
                                      pt[kc % 3][:, 0:512], start=st_, stop=sp_,
                                      tile_position=(0, 0)),
                        tensor.matmul(psS[32:33, 0:512], colsb[:, 0:1],
                                      pt[kc % 3][:, 512:1024], start=st_, stop=sp_,
                                      tile_position=(0, 32), skip_group_check=True),
                    ):
                        ii.then_inc(s_av.h, 1)

                for qb in range(NTB):
                    nkc = 4 * (qb + 1)
                    for kc in range(nkc):
                        emit_sc(qb, kc)
                        if kc >= 2:
                            emit_av(qb, kc - 2, nkc)
                    emit_av(qb, nkc - 2, nkc)
                    emit_av(qb, nkc - 1, nkc)
                    tensor.wait_ge(s_ev.h, M["rcp", qb])
                    tensor.wait_ge(s_ea.h, M["yev", qb])
                    tensor.matmul(psS[:, 0:512], rowf[0:1, 512:640], rsum[0:1, 0:512],
                                  start=True, stop=False)
                    tensor.matmul(psS[:, 0:512], rowf[0:1, 640:768],
                                  rsum[0:1, 512:1024], start=False,
                                  stop=True).then_inc(s_mm.h, 1)
            em.on("tensor", p4_tensor)

            def p4_scalar(scalar):
                for qb in range(NTB):
                    nkc = 4 * (qb + 1)
                    for kc in range(nkc):
                        scalar.wait_ge(s_sc.h, M["sc", qb, kc])
                        if qb == 0 and kc == 0:
                            scalar.wait_ge(s_ea.h, bcp_hp)
                        scalar.wait_ge(s_av.h, M["av", qb, kc - 3] if kc >= 3
                                       else (M["lastav", qb - 1] if qb else av0))
                        scalar.activation(out=pt[kc % 3][:, 0:512], in_=psA[kc % 3][:],
                                          func=AF.Exp, bias=bias16[:, 2 * qb:2 * qb + 1])
                        scalar.activation(out=pt[kc % 3][:, 512:1024],
                                          in_=psB[kc % 3][:], func=AF.Exp,
                                          bias=bias16[:, 2 * qb + 1:2 * qb + 2]
                                          ).then_inc(s_ea.h, 1)
            em.on("scalar", p4_scalar)

            def p4_vector(vector):
                for qb in range(NTB):
                    nkc = 4 * (qb + 1)
                    for kc in range(4 * qb, nkc):
                        pat = kc - 4 * qb
                        vector.wait_ge(s_ea.h, M["exp", qb, kc])
                        vector.tensor_mul(pt[kc % 3][:, 0:512], pt[kc % 3][:, 0:512],
                                          mask[:, 512 * pat:512 * (pat + 1)])
                        vector.tensor_mul(pt[kc % 3][:, 512:1024],
                                          pt[kc % 3][:, 512:1024],
                                          mask[:, 512 * pat:512 * (pat + 1)]
                                          ).then_inc(s_ev.h, 1)
                    vector.wait_ge(s_av.h, M["lastav", qb])
                    vector.tensor_copy(ysc[:], psY[:]).then_inc(s_ea.h, 1)
                    vector.tensor_copy(rsum[0:1, 0:512], psS[0:1, 0:512])
                    vector.tensor_copy(rsum[0:1, 512:1024],
                                       psS[32:33, 0:512]).then_inc(s_ev.h, 1)
                    if qb == 0:
                        vector.wait_ge(s_out.h, out0)
                    vector.wait_ge(s_mm.h, M["rb", qb])
                    vector.reciprocal(rr[:], psS[:, 0:512])
                    vector.drain()
                    vector.tensor_mul(y_sb[:, 512 * qb:512 * (qb + 1)], ysc[:],
                                      rr[:]).then_inc(s_ev.h, 1)
            em.on("vector", p4_vector)
            s_ea.n, s_ev.n, s_mm.n, s_sc.n, s_av.n = ea, ev, mm, sc, av

            def pout(gpsimd, hp=hp):
                s_ev.wait(gpsimd)
                gpsimd.dma_start(out=y_d[P * hp:P * (hp + 1), :], in_=y_sb[:]).then_inc(s_out.h, 16)
                s_out.n += 16
                s_out.wait(gpsimd)
            em.on("gpsimd", pout)

    S.barrier()




def io_stage_b(nc, T=T, pref="", ki="ExternalInput", ko="ExternalOutput"):
    io = {}
    io["b_xn_fm"] = nc.dram_tensor(pref + "b_xn_fm", [D, T], BF16, kind=ki).ap()
    for n in "qkv":
        io[f"b_w{n}"] = nc.dram_tensor(f"b_w{n}", [D, 512], F32, kind=ki).ap()
    for n in "qkv":
        io[f"b_b{n}"] = nc.dram_tensor(f"b_b{n}", [1, 512], F32, kind=ki).ap()
    io["cc_mask"] = nc.dram_tensor("cc_mask", [P, 2048], BF16, kind=ki).ap()
    io["cc_cols_bf"] = nc.dram_tensor("cc_cols_bf", [P, 4], BF16, kind=ki).ap()
    io["cc_row_f32"] = nc.dram_tensor("cc_row_f32", [1, 768], F32, kind=ki).ap()
    io["b_y"] = nc.dram_tensor(pref + "b_y", [512, T], BF16, kind=ko).ap()
    return io


def build_stage_b(T=T):
    nc = bass.Bass("TRN2", target_bir_lowering=False, debug=False)
    io = io_stage_b(nc, T=T)
    es = ExitStack()
    with es, nc.Block() as block:
        S = Shared(nc, block, es)
        emit_stage_b(S, io, T=T)
    return nc


WINDOW = 512
CHALO = 1536


def c_consts():
    c = {}
    i = np.arange(P)[:, None]
    j = np.arange(P)[None, :]
    am = np.zeros((P, 256), np.float32)
    am[:, 0:128] = np.where(j < i, NEG_BIG, 0.0)
    am[:, 128:256] = np.where(j > i, NEG_BIG, 0.0)
    c["cc_addmask"] = am
    rp = np.zeros((P, P), np.float32)
    for m in range(P):
        k2 = m + 32 if (m % 64) < 32 else m - 32
        rp[k2, m] = 1.0
    c["cc_rot"] = rp.astype(BF16NP)
    return c


def c_rotary_tables(t0):
    DIM_HEAD, WIN = 64, WINDOW
    n_total = 4096
    half = np.arange(0, DIM_HEAD, 2, dtype=np.float64)
    inv_freq = 1.0 / 10000.0 ** (half / DIM_HEAD)
    base = (half + 0.4 * DIM_HEAD) / (1.4 * DIM_HEAD)

    def tables(nbuf):
        t = np.arange(t0 - CHALO, t0 - CHALO + nbuf, dtype=np.float64)
        fr = t[:, None] * inv_freq[None, :]
        power = (t - n_total // 2) / float(WIN // 2)
        sc = base[None, :] ** power[:, None]
        cos, sin = np.cos(fr), np.sin(fr)
        def fm(a):
            x = np.concatenate([a.T, a.T], axis=0)
            return np.concatenate([x, x], axis=0).astype(np.float32)
        c1q = fm(cos * sc * 8.0)
        c1k = fm(cos / sc)
        def fm2(a):
            x = np.concatenate([-a.T, a.T], axis=0)
            return np.concatenate([x, x], axis=0).astype(np.float32)
        c2q = fm2(sin * sc * 8.0)
        c2k = fm2(sin / sc)
        return {"rq1": c1q, "rq2": c2q, "rk1": c1k, "rk2": c2k}
    return tables


def emit_stage_c_attn(S, io, nwin_own=4):
    OWN = WINDOW * nwin_own
    BUF = CHALO + OWN
    NW = nwin_own + 1
    KV0, Q0 = WINDOW, 2 * WINDOW
    NKT = (BUF - KV0) // 512
    NQT = (BUF - Q0) // 512
    NVT = (BUF - KV0) // P


    nc = S.nc
    block = S.block
    xin_d = io["c_xin_fm"]
    wqkv_d = io["c_wqkv"]
    wo_d = io["c_wo"]
    qs_d = io["c_qs"]
    ks_d = io["c_ks"]
    rot_d = {n: io[f"c_{n}"] for n in ("rq1", "rq2", "rk1", "rk2")}
    w0_d = io["c_w0col"]
    am_d = io["cc_addmask"]
    rp_d = io["cc_rot"]
    colsb_d = io["cc_cols_bf"]
    rowf_d = io["cc_row_f32"]
    ident_d = io["cc_ident"]
    x1_d = io["c_x1"]
    cx = StageCtx(S)
    xbt = [cx.sb(f"xbt{p}", [P, 8 * 516], BF16) for p in range(2)]
    wq_sb = cx.sb("wq_sb", [P, 8 * 1536], BF16)
    wo_sb = cx.sb("wo_sb", [P, 4 * D], BF16)
    qf = [cx.sb(f"qf{p}", [P, BUF - Q0], BF16) for p in range(4)]
    kf = [cx.sb(f"kf{p}", [P, BUF - KV0], BF16) for p in range(4)]
    v_sb = cx.sb("v_sb", [P, NVT * 512], BF16)
    y_sb = [cx.sb(f"y_sb{p}", [P, NW * 512], BF16) for p in range(4)]
    pt_sb = cx.sb("pt_sb", [P, 8 * 640], BF16)
    ptT_sb = cx.sb("ptT_sb", [P, 512], BF16)
    zro_sb = cx.sb("zro_sb", [P, P], BF16)
    qs_c = cx.sb("qs_c", [P, 1], F32)
    ks_c = cx.sb("ks_c", [P, 1], F32)
    w0_c = cx.sb("w0_c", [P, 2], F32)
    am_c = cx.sb("am_c", [P, 256], F32)
    rp_c = cx.sb("rp_c", [P, P], BF16)
    u_bf = cx.sb("u_bf", [P, 512], BF16)
    colsb = cx.sb("colsb", [P, 4], BF16)
    rowf = cx.sb("rowf", [1, 768], F32)
    ident = cx.sb("ident", [P, P], BF16)
    rt = [{n: cx.sb(f"rt{p}_{n}", [P, 512], F32) for n in ("rq1", "rq2", "rk1", "rk2")}
          for p in range(2)]
    sqs = cx.sb("sqs", [P, 512], BF16)
    rrow = cx.sb("rrow", [1, 2048], F32)
    u_t = cx.sb("u_t", [P, 512], F32)
    rr2 = cx.sb("rr2", [P, 512], F32)
    t1_t = cx.sb("t1_t", [P, 512], F32)
    t2_t = cx.sb("t2_t", [P, 512], F32)
    kcmd = [cx.sb(f"kcmd{p}", [P, (BUF - KV0) // 128], F32) for p in range(4)]
    kid = cx.sb("kid", [P, 1], BF16)
    bias_all = cx.sb("bias_all", [P, (BUF - Q0) // 16], F32)
    bias_sb = cx.sb("bias_sb", [P, 2], F32)
    sums_c = cx.sb("sums_c", [P, 2], F32)
    rcp_c = cx.sb("rcp_c", [P, 2], F32)
    rcp_scr = cx.sb("rcp_scr", [P, 2], F32)
    xsl = [cx.sb(f"xsl{p}", [P, 512], F32) for p in range(2)]

    PS640A = cx.ps("PS640A", [P, 640])
    PS640B = cx.ps("PS640B", [P, 640])
    PSbf = cx.ps("PSbf", [P, 512], BF16)
    PSY = cx.ps("PSY", [P, 512])
    PSG = cx.ps("PSG", [P, 512])
    PSM = cx.ps("PSM", [P, 512])

    ldg = cx.sem("ldg")
    ldc = cx.sem("ldc")
    ldr = DmaStream(cx, "ldr", 2)
    ldxs = DmaStream(cx, "ldxs", 2)
    s_mm = cx.sem("s_mm")
    s_ea = cx.sem("s_ea")
    s_ev = cx.sem("s_ev")
    s_av = cx.sem("s_av")
    s_out = cx.sem("s_out")
    SEMS = [ldg, ldc, s_mm, s_ea, s_ev, s_av, s_out]

    def barrier(eng):
        for s in SEMS:
            if s.n:
                s.wait(eng)

    with cx.es:
        em = S.em

        def p0_gp(gpsimd):
            for kc in range(8):
                gpsimd.dma_start(out=wq_sb[:, 1536 * kc:1536 * (kc + 1)],
                                 in_=wqkv_d[P * kc:P * (kc + 1), :]).then_inc(ldg.h, 16)
                ldg.n += 16
            for ic in range(4):
                gpsimd.dma_start(out=wo_sb[:, D * ic:D * (ic + 1)],
                                 in_=wo_d[P * ic:P * (ic + 1), :]).then_inc(ldg.h, 16)
                ldg.n += 16
        em.on("gpsimd", p0_gp)

        def p0_zero(vector):
            s_ev.inc(vector.memset(zro_sb[:], 0.0))
        em.on("vector", p0_zero)

        def p0_sync(sync):
            for dst, src in ((qs_c, qs_d), (ks_c, ks_d), (w0_c, w0_d), (am_c, am_d),
                             (rp_c, rp_d), (colsb, colsb_d), (rowf, rowf_d),
                             (ident, ident_d)):
                sync.dma_start(out=dst[:], in_=src[:, :]).then_inc(ldc.h, 16)
                ldc.n += 16
        em.on("sync", p0_sync)

        ldxt = DmaStream(cx, "ldxt", 2)
        pcnt = [0]

        def xsl_t(par, c, off, n):
            o = 0 if c >= 4 else 1
            return xbt[par][:, 516 * c + off + o:516 * c + off + o + n]

        NTBK = (BUF - KV0) // 512
        for tbi in range(NTBK):
            tb0 = KV0 + 512 * tbi
            par = tbi % 2

            def p1_ld(gpsimd, tbi=tbi, tb0=tb0, par=par):
                if tbi >= 2:
                    s_ea.wait(gpsimd, s_ea.n)
                    s_ev.wait(gpsimd, s_ev.n)
                for c in range(8):
                    ldxt.issue(gpsimd, tbi, xbt[par][:, 516 * c:516 * c + 513],
                               xin_d[P * c:P * (c + 1), tb0 - 1:tb0 + 512])
            em.on("gpsimd", p1_ld)

            def p1_wait(tensor, tbi=tbi, par=par):
                if tbi == 0:
                    ldg.wait(tensor)
                    ldc.wait(tensor)
                tensor.wait_ge(ldxt.sems[par].h, ldxt.base[par] + 128 * (tbi // 2 + 1))
            em.on("tensor", p1_wait)

            jobs = [("k", pr) for pr in range(4)]
            if tb0 >= Q0:
                jobs += [("q", pr) for pr in range(4)]
            jobs += [("v", st) for st in range(4)]
            for kind, pr in jobs:
                def p1_pe(tensor, kind=kind, pr=pr, tbi=tbi, tb0=tb0, par=par):
                    idx = pcnt[0]
                    if idx >= 2:
                        s_ea.wait(tensor, s_ea.n - 1)
                    ps = PSG if idx % 2 == 0 else PSY
                    if kind == "v":
                        for kc in range(8):
                            ins = tensor.matmul(ps[:], xsl_t(par, kc, 128 * pr, P),
                                                wq_sb[:, 1536 * kc + 1024:1536 * kc + 1536],
                                                start=(kc == 0), stop=(kc == 7))
                    else:
                        wcol0 = 0 if kind == "q" else 512
                        for kc in range(8):
                            ins = tensor.matmul(
                                ps[:],
                                wq_sb[:, 1536 * kc + wcol0 + 128 * pr:1536 * kc + wcol0 + 128 * (pr + 1)],
                                xsl_t(par, kc, 0, 512), start=(kc == 0), stop=(kc == 7))
                    s_mm.inc(ins)
                    pcnt[0] += 1
                em.on("tensor", p1_pe)

                def p1_ev(vector, kind=kind, pr=pr, tbi=tbi, tb0=tb0):
                    s_mm.wait(vector)
                    ps = PSG if (pcnt[0] - 1) % 2 == 0 else PSY
                    if kind == "v":
                        tt = 4 * tbi + pr
                        s_ea.inc(vector.tensor_copy(v_sb[:, 512 * tt:512 * (tt + 1)], ps[:]))
                    elif kind == "q":
                        rel = tb0 - Q0
                        s_ea.inc(vector.tensor_copy(qf[pr][:, rel:rel + 512], ps[:]))
                    else:
                        rel = tb0 - KV0
                        s_ea.inc(vector.tensor_copy(kf[pr][:, rel:rel + 512], ps[:]))
                em.on("vector", p1_ev)

        for tb in range(NKT):
            col0 = KV0 + 512 * tb
            def p2_ld(sync, tb=tb, col0=col0):
                if tb >= 2:
                    s_ev.wait(sync, s_ev.n)
                for i, n in enumerate(("rq1", "rq2", "rk1", "rk2")):
                    ldr.issue(sync, tb, rt[tb % 2][n][:],
                              rot_d[n][:, col0:col0 + 512])
            em.on("sync", p2_ld)

            for qk, (src_list, scol, rta, rtb_, nrng0) in enumerate(
                    ((qf, qs_c, "rq1", "rq2", Q0), (kf, ks_c, "rk1", "rk2", KV0))):
                if col0 < nrng0:
                    continue
                rel0 = col0 - nrng0
                for pair in range(4):
                    def p2_work(pair=pair, qk=qk, tb=tb, col0=col0, rel0=rel0,
                                src_list=src_list, scol=scol, rta=rta, rtb_=rtb_):
                        src = src_list[pair]
                        sl = src[:, rel0:rel0 + 512]

                        def w_sq(vector):
                            s_ea.wait(vector)
                            s_mm.wait(vector)
                            s_ea.inc(vector.tensor_mul(sqs[:], sl, sl))
                        em.on("vector", w_sq)

                        def w_ssq(tensor):
                            s_ea.wait(tensor)
                            s_ev.wait(tensor)
                            i1 = tensor.matmul(PSG[0:1, 0:512], colsb[:, 1:2], sqs[:],
                                               start=True, stop=True, tile_position=(0, 0))
                            i2 = tensor.matmul(PSG[32:33, 0:512], colsb[:, 2:3], sqs[:],
                                               start=True, stop=True, tile_position=(0, 32),
                                               skip_group_check=True)
                            s_mm.inc(i1)
                            s_mm.inc(i2)
                        em.on("tensor", w_ssq)

                        def w_mx(vector):
                            s_mm.wait(vector)
                            vector.tensor_scalar_max(rrow[0:1, 0:512], PSG[0:1, 0:512],
                                                     1e-24)
                            vector.drain()
                            s_ev.inc(vector.tensor_scalar_max(rrow[0:1, 512:1024],
                                                              PSG[32:33, 0:512], 1e-24))
                            vector.drain()
                        em.on("vector", w_mx)

                        def w_rs(scalar):
                            s_ev.wait(scalar)
                            s_ea.inc(scalar.activation(out=rrow[0:1, 1024:2048],
                                                       in_=rrow[0:1, 0:1024],
                                                       func=AF.Sqrt))
                        em.on("scalar", w_rs)

                        def w_bc(tensor):
                            s_ea.wait(tensor)
                            tensor.matmul(PSM[:, 0:512], rowf[0:1, 512:640],
                                          rrow[0:1, 1024:1536], start=True, stop=False)
                            s_mm.inc(tensor.matmul(PSM[:, 0:512], rowf[0:1, 640:768],
                                                   rrow[0:1, 1536:2048], start=False, stop=True))
                        em.on("tensor", w_bc)

                        def w_u(vector):
                            s_mm.wait(vector)
                            vector.reciprocal(rr2[:], PSM[:, 0:512])
                            vector.drain()
                            vector.scalar_tensor_tensor(u_t[:], sl, scol[:, 0:1],
                                                        rr2[:], ALU.mult, ALU.mult)
                            vector.drain()
                            s_ev.inc(vector.tensor_copy(u_bf[:], u_t[:]))
                        em.on("vector", w_u)

                        def w_perm(tensor):
                            s_ev.wait(tensor)
                            s_mm.inc(tensor.matmul(PS640A[:, 0:512], rp_c[:], u_bf[:],
                                                   start=True, stop=True))
                        em.on("tensor", w_perm)

                        def w_rot(vector, tb=tb, rta=rta, rtb_=rtb_, sl=sl):
                            s_mm.wait(vector)
                            vector.wait_ge(ldr.sems[tb % 2].h, ldr.base[tb % 2] + 64 * (tb // 2 + 1))
                            r1 = rt[tb % 2][rta]
                            r2 = rt[tb % 2][rtb_]
                            vector.tensor_mul(t1_t[:], u_t[:], r1[:])
                            vector.tensor_mul(t2_t[:], PS640A[:, 0:512], r2[:])
                            vector.drain()
                            s_ev.inc(vector.tensor_add(sl, t1_t[:], t2_t[:]))
                        em.on("vector", w_rot)

                    p2_work()

        NR_K = BUF - KV0
        NR_Q = BUF - Q0
        NCH = NR_K // 128
        NWQ = NR_Q // 512
        for pair in range(4):
            def p25_kcm(vector, pair=pair):
                s_ev.wait(vector)
                s_ea.wait(vector)
                for ch in range(NCH):
                    vector.reduce_max(kcmd[pair][:, ch:ch + 1],
                                      kf[pair][:, 128 * ch:128 * (ch + 1)],
                                      axis=AX.X, apply_absolute_value=True)
                vector.drain()
                s_ev.inc(vector.nop())
            em.on("vector", p25_kcm)

            for wi in range(NWQ):
                def p25_abs(scalar, pair=pair, wi=wi):
                    s_ev.wait(scalar)
                    s_mm.wait(scalar)
                    s_ea.inc(scalar.activation(out=sqs[:],
                                               in_=qf[pair][:, 512 * wi:512 * (wi + 1)],
                                               func=AF.Abs))
                em.on("scalar", p25_abs)

                for qs_i in range(4):
                    def p25_kid(vector, pair=pair, wi=wi, qs_i=qs_i):
                        s_ev.wait(vector)
                        s_mm.wait(vector)
                        ch0 = 4 * wi + qs_i
                        vector.reduce_max(kid[:, 0:1], kcmd[pair][:, ch0:ch0 + 5],
                                          axis=AX.X)
                        vector.drain()
                        s_ev.inc(vector.tensor_scalar_mul(kid[:, 0:1], kid[:, 0:1], -1.0))
                        vector.drain()
                    em.on("vector", p25_kid)

                    def p25_mm(tensor, pair=pair, wi=wi, qs_i=qs_i):
                        s_ev.wait(tensor)
                        s_ea.wait(tensor)
                        s_mm.inc(tensor.matmul(PSM[:, 0:1],
                                               sqs[0:64, 128 * qs_i:128 * (qs_i + 1)],
                                               kid[0:64, 0:1], start=True, stop=True))
                        s_mm.inc(tensor.matmul(PSG[:, 0:1],
                                               sqs[64:128, 128 * qs_i:128 * (qs_i + 1)],
                                               kid[64:128, 0:1], start=True, stop=True,
                                               tile_position=(64, 0)))
                    em.on("tensor", p25_mm)

                    def p25_cp(vector, pair=pair, wi=wi, qs_i=qs_i):
                        s_mm.wait(vector)
                        idx = 2 * (pair * NWQ * 4 + wi * 4 + qs_i)
                        vector.tensor_copy(bias_all[:, idx:idx + 1], PSM[:, 0:1])
                        vector.drain()
                        s_ea.inc(vector.tensor_copy(bias_all[:, idx + 1:idx + 2],
                                                    PSG[:, 0:1]))
                        vector.drain()
                    em.on("vector", p25_cp)

        ptfree_mark = [s_mm.n]
        for pair in range(4):
            for wi in range(NW):
                krel0 = 512 * wi
                qrel0 = 512 * wi

                for qs_i in range(4):
                    def p3_sc(tensor, pair=pair, wi=wi, qs_i=qs_i, qrel0=qrel0, krel0=krel0):
                        s_ea.wait(tensor)
                        kbase = krel0 + 128 * qs_i
                        qb0 = qrel0 + 128 * qs_i
                        for hh, (ps, tp) in enumerate(((PS640A, (0, 0)), (PS640B, (64, 0)))):
                            r0 = 64 * hh
                            tensor.matmul(ps[:, 0:512],
                                          qf[pair][r0:r0 + 64, qb0:qb0 + 128],
                                          kf[pair][r0:r0 + 64, kbase:kbase + 512],
                                          start=True, stop=True, tile_position=tp)
                            s_mm.inc(tensor.matmul(
                                ps[:, 512:640],
                                qf[pair][r0:r0 + 64, qb0:qb0 + 128],
                                kf[pair][r0:r0 + 64, kbase + 512:kbase + 640],
                                start=True, stop=True, tile_position=tp,
                                skip_group_check=True))
                    em.on("tensor", p3_sc)
                    sc_mark = s_mm.n

                    def p3_msk(vector, wi=wi, qs_i=qs_i, sc_mark=sc_mark):
                        vector.wait_ge(s_mm.h, sc_mark)
                        for ps in (PS640A, PS640B):
                            vector.tensor_add(ps[:, 0:128], ps[:, 0:128], am_c[:, 0:128])
                            ins = vector.tensor_add(ps[:, 512:640], ps[:, 512:640],
                                                    am_c[:, 128:256])
                            if wi <= 1:
                                wdt = 512 - 128 * qs_i
                                vector.drain()
                                ins = vector.tensor_scalar_add(ps[:, 0:wdt], ps[:, 0:wdt],
                                                               w0_c[:, wi:wi + 1])
                            s_ev.inc(ins)
                    em.on("vector", p3_msk)
                    msk_mark = s_ev.n

                    def p3_exp(scalar, pair=pair, wi=wi, qs_i=qs_i, msk_mark=msk_mark):
                        scalar.wait_ge(s_ev.h, msk_mark)
                        scalar.wait_ge(s_mm.h, ptfree_mark[0])
                        scalar.wait_ge(s_ev.h, s_ev.n)
                        idx = 2 * (pair * NWQ * 4 + wi * 4 + qs_i)
                        for hh, ps in enumerate((PS640A, PS640B)):
                            o = 640 * (4 * hh + qs_i)
                            s_ea.inc(scalar.activation(out=pt_sb[:, o:o + 640], in_=ps[:],
                                                       func=AF.Exp, bias=bias_all[:, idx + hh:idx + hh + 1],
                                                       accum_out=sums_c[:, hh:hh + 1]))
                    em.on("scalar", p3_exp)
                    exp_mark = s_ea.n

                    def p3_nrm(vector, qs_i=qs_i, exp_mark=exp_mark):
                        vector.wait_ge(s_ea.h, exp_mark)
                        recip_approx(vector, rcp_c[:, 0:2], sums_c[:, 0:2],
                                     rcp_scr[:, 0:2])
                        for hh in range(2):
                            o = 640 * (4 * hh + qs_i)
                            s_ev.inc(vector.tensor_scalar_mul(pt_sb[:, o:o + 640],
                                                              pt_sb[:, o:o + 640],
                                                              rcp_c[:, hh:hh + 1]))
                    em.on("vector", p3_nrm)

                nrm_mark = s_ev.n
                for sc_ in range(8):
                    for hh in range(2):
                        def p3_tr(tensor, pair=pair, wi=wi, sc_=sc_, hh=hh, nrm_mark=nrm_mark):
                            if sc_ == 0 and hh == 0:
                                s_ev.wait(tensor, nrm_mark)
                                ldc.wait(tensor)
                            s_ea.wait(tensor, s_ea.n)
                            for qs_i in range(4):
                                rel = sc_ - qs_i
                                if 0 <= rel <= 4:
                                    srcap = pt_sb[:, 640 * (4 * hh + qs_i) + 128 * rel:
                                                  640 * (4 * hh + qs_i) + 128 * (rel + 1)]
                                else:
                                    srcap = zro_sb[:]
                                s_mm.inc(tensor.transpose(
                                    PSbf[:, 128 * qs_i:128 * (qs_i + 1)], srcap, ident[:]))
                        em.on("tensor", p3_tr)

                        def p3_ptev(vector, sc_=sc_, hh=hh):
                            s_mm.wait(vector)
                            s_av.wait(vector)
                            s_ea.inc(vector.tensor_copy(ptT_sb[:], PSbf[:]))
                        em.on("vector", p3_ptev)
                        ptev_mark = s_ea.n

                        def p3_av(tensor, pair=pair, wi=wi, sc_=sc_, hh=hh,
                                  ptev_mark=ptev_mark):
                            tensor.wait_ge(s_ea.h, ptev_mark)
                            if sc_ == 0 and hh == 0:
                                s_ea.wait(tensor, s_ea.n)
                            vtile = 4 * wi + sc_
                            c0 = 512 * vtile + 64 * (2 * pair + hh)
                            s_av.inc(tensor.matmul(
                                PSY[64 * hh:64 * (hh + 1), :], v_sb[:, c0:c0 + 64],
                                ptT_sb[:], start=(sc_ == 0), stop=(sc_ == 7),
                                tile_position=(0, 64 * hh),
                                skip_group_check=(hh == 1)))
                        em.on("tensor", p3_av)
                ptfree_mark[0] = s_mm.n

                def p3_yev(vector, pair=pair, wi=wi):
                    s_av.wait(vector)
                    s_ea.inc(vector.tensor_copy(y_sb[pair][:, 512 * wi:512 * (wi + 1)],
                                                PSY[:]))
                em.on("vector", p3_yev)

        pq = [0]
        for oc in range(8):
            for tb in range(NW):
                def p4_ld(sync, oc=oc, tb=tb):
                    i = pq[0]
                    if i >= 2:
                        s_ev.wait(sync, s_ev.n - 1)
                        s_out.wait(sync)
                    ldxs.issue(sync, i, xsl[i % 2][:],
                               xin_d[P * oc:P * (oc + 1),
                                     Q0 + 512 * tb:Q0 + 512 * (tb + 1)])
                em.on("sync", p4_ld)

                def p4_mm(tensor, oc=oc, tb=tb):
                    if pq[0] == 0:
                        s_ea.wait(tensor)
                    if pq[0] >= 2:
                        s_ev.wait(tensor, s_ev.n - 1)
                    ps = PSG if pq[0] % 2 == 0 else PSY
                    for ic in range(4):
                        ins = tensor.matmul(ps[:], wo_sb[:, D * ic + P * oc:D * ic + P * (oc + 1)],
                                            y_sb[ic][:, 512 * tb:512 * (tb + 1)],
                                            start=(ic == 0), stop=(ic == 3))
                    s_mm.inc(ins)
                em.on("tensor", p4_mm)

                def p4_add(vector, oc=oc, tb=tb):
                    i = pq[0]
                    s_mm.wait(vector)
                    ldxs.wait(vector, i)
                    ps = PSG if i % 2 == 0 else PSY
                    s_ev.inc(vector.tensor_add(xsl[i % 2][:], xsl[i % 2][:], ps[:]))
                em.on("vector", p4_add)

                def p4_st(gpsimd, oc=oc, tb=tb):
                    s_ev.wait(gpsimd)
                    gpsimd.dma_start(out=x1_d[P * oc:P * (oc + 1), 512 * tb:512 * (tb + 1)],
                                     in_=xsl[pq[0] % 2][:]).then_inc(s_out.h, 16)
                    s_out.n += 16
                em.on("gpsimd", p4_st)
                pq[0] += 1

        def fin(gpsimd):
            s_out.wait(gpsimd)
        em.on("gpsimd", fin)

    S.barrier()






def io_stage_c_attn(nc, nwin_own=4, pref="", ki="ExternalInput", ko="ExternalOutput"):
    OWN = WINDOW * nwin_own
    BUF = CHALO + OWN
    io = {}
    io["c_xin_fm"] = nc.dram_tensor(pref + "c_xin_fm", [D, BUF], F32, kind=ki).ap()
    io["c_wqkv"] = nc.dram_tensor(pref + "c_wqkv", [D, 1536], F32, kind=ki).ap()
    io["c_wo"] = nc.dram_tensor(pref + "c_wo", [512, D], F32, kind=ki).ap()
    io["c_qs"] = nc.dram_tensor(pref + "c_qs", [P, 1], F32, kind=ki).ap()
    io["c_ks"] = nc.dram_tensor(pref + "c_ks", [P, 1], F32, kind=ki).ap()
    for n in ("rq1", "rq2", "rk1", "rk2"):
        io[f"c_{n}"] = nc.dram_tensor(f"c_{n}", [P, BUF], F32, kind=ki).ap()
    io["c_w0col"] = nc.dram_tensor(pref + "c_w0col", [P, 2], F32, kind=ki).ap()
    io["cc_addmask"] = nc.dram_tensor("cc_addmask", [P, 256], F32, kind=ki).ap()
    io["cc_rot"] = nc.dram_tensor("cc_rot", [P, P], BF16, kind=ki).ap()
    io["cc_cols_bf"] = nc.dram_tensor("cc_cols_bf", [P, 4], BF16, kind=ki).ap()
    io["cc_row_f32"] = nc.dram_tensor("cc_row_f32", [1, 768], F32, kind=ki).ap()
    io["cc_ident"] = nc.dram_tensor("cc_ident", [P, P], BF16, kind=ki).ap()
    io["c_x1"] = nc.dram_tensor(pref + "c_x1", [D, WINDOW + OWN], F32, kind=ko).ap()
    return io


def build_stage_c_attn(nwin_own=4):
    nc = bass.Bass("TRN2", target_bir_lowering=False, debug=False)
    io = io_stage_c_attn(nc, nwin_own=nwin_own)
    es = ExitStack()
    with es, nc.Block() as block:
        S = Shared(nc, block, es)
        emit_stage_c_attn(S, io, nwin_own=nwin_own)
    return nc


def emit_stage_c_ffn(S, io, phases=3):
    HID = 2730
    NMP = 22
    OWN = 2048
    NTB = OWN // 512

    nc = S.nc
    block = S.block
    x1_d = io["cf_x1"]
    g_d = io["cf_g"]
    b_d = io["cf_b"]
    w1_d = io["cf_w1"]
    b1_d = io["cf_b1"]
    w2_d = io["cf_w2"]
    b2_d = io["cf_b2"]
    colsf_d = io["cc_cols_f32"]
    rowf_d = io["cc_row_f32"]
    out_d = io["cf_out"]
    cx = StageCtx(S)
    xt1 = cx.sb("xt1", [P, 8 * 516], F32)
    hn = cx.sb("hn", [P, 8 * OWN], BF16)
    hg = [cx.sb(f"hg{i}", [P, OWN], BF16) for i in range(NMP)]
    w1t = [cx.sb(f"w1t{p}", [P, 8 * 256], BF16) for p in range(2)]
    w2t = [cx.sb(f"w2t{p}", [P, NMP * P], BF16) for p in range(2)]
    b1c = cx.sb("b1c", [P, 4], F32)
    gcol = cx.sb("gcol", [P, 8], F32)
    bcol = cx.sb("bcol", [P, 8], F32)
    b2col = cx.sb("b2col", [P, 8], F32)
    colsf = cx.sb("colsf", [P, 4], F32)
    rowf = cx.sb("rowf", [1, 768], F32)
    sq = cx.sb("sq", [P, 512], F32)
    rows = cx.sb("rows", [1, 8 * 512], F32)
    tmp1 = cx.sb("tmp1", [P, 512], F32)
    glu = cx.sb("glu", [P, 512], F32)
    asb = cx.sb("asb", [P, 512], F32)
    xsl = [cx.sb(f"xsl{p}", [P, 512], F32) for p in range(2)]
    osb = [cx.sb(f"osb{p}", [P, 512], F32) for p in range(2)]
    PSA = cx.ps("PSA", [P, 512])
    PSB = cx.ps("PSB", [P, 512])
    PSR = cx.ps("PSR", [P, 512])
    PSR2 = cx.ps("PSR2", [P, 512])

    ldc = cx.sem("ldc")
    ldx = DmaStream(cx, "ldx", 1)
    ldw1 = DmaStream(cx, "ldw1", 2)
    ldb1 = DmaStream(cx, "ldb1", 2)
    ldw2 = DmaStream(cx, "ldw2", 2)
    ldxs = DmaStream(cx, "ldxs", 2)
    s_mm = cx.sem("s_mm")
    s_ea = cx.sem("s_ea")
    s_ev = cx.sem("s_ev")
    s_out = cx.sem("s_out")

    with cx.es:
        em = S.em

        def p0(sync):
            for dst, srcd in ((gcol, g_d), (bcol, b_d), (b2col, b2_d), (colsf, colsf_d)):
                sync.dma_start(out=dst[:], in_=srcd[:, :]).then_inc(ldc.h, 16)
                ldc.n += 16
            sync.dma_start(out=rowf[:], in_=rowf_d[:, :]).then_inc(ldc.h, 16)
            ldc.n += 16
        em.on("sync", p0)

        def xs(c, w=512):
            o = 0 if c >= 4 else 1
            return xt1[:, 516 * c + o:516 * c + o + w]

        for tb in range(NTB):
            def f1_ld(gpsimd, tb=tb):
                if tb >= 1:
                    s_ev.wait(gpsimd)
                for c in range(8):
                    ldx.issue(gpsimd, 8 * tb + c, xt1[:, 516 * c:516 * c + 513],
                              x1_d[P * c:P * (c + 1), 511 + 512 * tb:1024 + 512 * tb])
            em.on("gpsimd", f1_ld)

            for c in range(8):
                def f1_sq(vector, tb=tb, c=c):
                    ldx.wait(vector, 8 * tb + 7)
                    s_mm.wait(vector)
                    s_ea.inc(vector.tensor_mul(sq[:], xs(c), xs(c)))
                em.on("vector", f1_sq)

                def f1_mm(tensor, tb=tb, c=c):
                    s_ea.wait(tensor)
                    if c == 0:
                        ldc.wait(tensor)
                        s_ev.wait(tensor)
                    i1 = tensor.matmul(PSR[0:1, 0:512], colsf[:, 0:1], xs(c),
                                       start=(c == 0), stop=(c == 7),
                                       tile_position=(0, 0))
                    i2 = tensor.matmul(PSR[32:33, 0:512], colsf[:, 0:1], sq[:],
                                       start=(c == 0), stop=(c == 7),
                                       tile_position=(0, 32), skip_group_check=True)
                    s_mm.inc(i1)
                    s_mm.inc(i2)
                em.on("tensor", f1_mm)

            def f1_rows(vector, tb=tb):
                s_mm.wait(vector)
                vector.tensor_scalar_mul(rows[0:1, 1024:1536], PSR[0:1, 0:512], -1.0 / D)
                vector.drain()
                vector.tensor_mul(rows[0:1, 2048:2560], rows[0:1, 1024:1536],
                                  rows[0:1, 1024:1536])
                vector.drain()
                vector.scalar_tensor_tensor(rows[0:1, 1536:2048], PSR[32:33, 0:512],
                                            1.0 / D, rows[0:1, 2048:2560],
                                            ALU.mult, ALU.subtract)
                vector.drain()
                s_ev.inc(vector.tensor_scalar_add(rows[0:1, 1536:2048],
                                                  rows[0:1, 1536:2048], 1e-5))
            em.on("vector", f1_rows)

            def f1_rstd(scalar, tb=tb):
                s_ev.wait(scalar)
                s_ea.inc(scalar.activation(out=rows[0:1, 2048:2560],
                                           in_=rows[0:1, 1536:2048], func=AF.Sqrt))
            em.on("scalar", f1_rstd)

            def f1_rcp(vector, tb=tb):
                s_ea.wait(vector)
                s_ev.inc(recip_approx(vector, rows[0:1, 2560:3072], rows[0:1, 2048:2560],
                                      rows[0:1, 3072:3584]))
            em.on("vector", f1_rcp)

            def f1_bcast(tensor, tb=tb):
                s_ea.wait(tensor)
                s_ev.wait(tensor)
                tensor.matmul(PSA[:, 0:512], rowf[0:1, 0:128], rows[0:1, 1024:1536],
                              start=True, stop=True)
                s_mm.inc(tensor.matmul(PSB[:, 0:512], rowf[0:1, 0:128],
                                       rows[0:1, 2560:3072], start=True, stop=True))
            em.on("tensor", f1_bcast)

            for c in range(8):
                def f1_ap(vector, tb=tb, c=c):
                    s_mm.wait(vector)
                    vector.drain()
                    vector.tensor_add(tmp1[:], xs(c), PSA[:, 0:512])
                    vector.drain()
                    vector.tensor_mul(tmp1[:], tmp1[:], PSB[:, 0:512])
                    vector.drain()
                    s_ev.inc(vector.tensor_scalar(
                        hn[:, OWN * c + 512 * tb:OWN * c + 512 * (tb + 1)], tmp1[:],
                        gcol[:, c:c + 1], bcol[:, c:c + 1], ALU.mult, ALU.add))
                em.on("vector", f1_ap)

        mmb2, eab2, evb2 = s_mm.n, s_ea.n, s_ev.n
        NF2 = NMP if phases >= 2 else 0
        for i in range(NF2):
            wid = min(P, HID - P * i)
            def f2_ld(gpsimd, i=i, wid=wid):
                if i >= 2:
                    s_mm.wait(gpsimd)
                for kc in range(8):
                    ldw1.issue(gpsimd, i, w1t[i % 2][:, 256 * kc:256 * kc + wid],
                               w1_d[P * kc:P * (kc + 1), P * i:P * i + wid])
                    ldw1.issue(gpsimd, i, w1t[i % 2][:, 256 * kc + 128:256 * kc + 128 + wid],
                               w1_d[P * kc:P * (kc + 1), HID + P * i:HID + P * i + wid])
                ldb1.issue(gpsimd, i, b1c[0:wid, 2 * (i % 2):2 * (i % 2) + 1],
                           b1_d[0:1, P * i:P * i + wid].rearrange("o p -> p o"))
                ldb1.issue(gpsimd, i, b1c[0:wid, 2 * (i % 2) + 1:2 * (i % 2) + 2],
                           b1_d[0:1, HID + P * i:HID + P * i + wid].rearrange("o p -> p o"))
            em.on("gpsimd", f2_ld)

            for tb in range(NTB):
                def f2_mm(tensor, i=i, tb=tb, wid=wid):
                    if i == 0 and tb == 0:
                        s_ev.wait(tensor)
                    if not (i == 0 and tb < 2):
                        s_ev.wait(tensor, s_ev.n - 1)
                    tensor.wait_ge(ldw1.sems[i % 2].h, ldw1.base[i % 2] + 16 * 16 * (i // 2 + 1))
                    pa = PSA if tb % 2 == 0 else PSR
                    pg = PSB if tb % 2 == 0 else PSR2
                    for kc in range(8):
                        tensor.matmul(pa[0:wid, :], w1t[i % 2][:, 256 * kc:256 * kc + wid],
                                      hn[:, OWN * kc + 512 * tb:OWN * kc + 512 * (tb + 1)],
                                      start=(kc == 0), stop=(kc == 7))
                        ins = tensor.matmul(pg[0:wid, :],
                                            w1t[i % 2][:, 256 * kc + 128:256 * kc + 128 + wid],
                                            hn[:, OWN * kc + 512 * tb:OWN * kc + 512 * (tb + 1)],
                                            start=(kc == 0), stop=(kc == 7))
                    s_mm.inc(ins)
                em.on("tensor", f2_mm)

                def f2_mul(vector, i=i, tb=tb, wid=wid):
                    k = NTB * i + tb
                    s_mm.wait(vector)
                    vector.wait_ge(ldb1.sems[i % 2].h, ldb1.base[i % 2] + 16 * 2 * (i // 2 + 1))
                    vector.drain()
                    pa = PSA if tb % 2 == 0 else PSR
                    vector.tensor_scalar(asb[0:wid, :], pa[0:wid, :],
                                         b1c[0:wid, 2 * (i % 2):2 * (i % 2) + 1], 0.0,
                                         ALU.add, ALU.bypass)
                    vector.drain()
                    vector.wait_ge(s_ea.h, eab2 + k + 1)
                    s_ev.inc(vector.tensor_mul(hg[i][0:wid, 512 * tb:512 * (tb + 1)],
                                               asb[0:wid, :], glu[0:wid, :]))
                em.on("vector", f2_mul)

        def f2_gelus(scalar):
            for i in range(NF2):
                wid = min(P, HID - P * i)
                for tb in range(NTB):
                    k = NTB * i + tb
                    scalar.wait_ge(s_mm.h, mmb2 + k + 1)
                    scalar.wait_ge(s_ev.h, evb2 + k)
                    scalar.wait_ge(ldb1.sems[i % 2].h,
                                   ldb1.base[i % 2] + 16 * 2 * (i // 2 + 1))
                    pg = PSB if tb % 2 == 0 else PSR2
                    scalar.activation(out=glu[0:wid, :], in_=pg[0:wid, :], func=AF.Gelu,
                                      bias=b1c[0:wid, 2 * (i % 2) + 1:2 * (i % 2) + 2]
                                      ).then_inc(s_ea.h, 1)
        if NF2:
            em.on("scalar", f2_gelus)
            s_ea.n += NF2 * NTB

        if phases == 1:
            def f1_out(gpsimd):
                s_ev.wait(gpsimd)
                for c in range(8):
                    gpsimd.dma_start(out=out_d[P * c:P * (c + 1), :],
                                     in_=hn[:, OWN * c:OWN * (c + 1)]).then_inc(s_out.h, 16)
                    s_out.n += 16
                s_out.wait(gpsimd)
            em.on("gpsimd", f1_out)
            cx.es.close()
            S.barrier()
            return
        if phases == 2:
            def f2_out(gpsimd):
                s_ev.wait(gpsimd)
                for c in range(8):
                    gpsimd.dma_start(out=out_d[P * c:P * (c + 1), :],
                                     in_=hg[c][:, :]).then_inc(s_out.h, 16)
                    s_out.n += 16
                s_out.wait(gpsimd)
            em.on("gpsimd", f2_out)
            cx.es.close()
            S.barrier()
            return

        for oc in range(8):
            def f3_ld(gpsimd, oc=oc):
                if oc >= 2:
                    s_mm.wait(gpsimd)
                for hc in range(NMP):
                    hw = min(P, HID - P * hc)
                    ldw2.issue(gpsimd, oc, w2t[oc % 2][0:hw, P * hc:P * (hc + 1)],
                               w2_d[P * hc:P * hc + hw, P * oc:P * (oc + 1)])
            em.on("gpsimd", f3_ld)

            for tb in range(NTB):
                def f3_ldx(sync, oc=oc, tb=tb):
                    i = 4 * oc + tb
                    if i >= 2:
                        s_ev.wait(sync, s_ev.n - 1)
                        s_out.wait(sync)
                    ldxs.issue(sync, i, xsl[i % 2][:],
                               x1_d[P * oc:P * (oc + 1), 512 + 512 * tb:1024 + 512 * tb])
                em.on("sync", f3_ldx)

                def f3_mm(tensor, oc=oc, tb=tb):
                    i = 4 * oc + tb
                    if i == 0:
                        s_ev.wait(tensor)
                    if i >= 2:
                        s_ev.wait(tensor, s_ev.n - 1)
                    tensor.wait_ge(ldw2.sems[oc % 2].h, ldw2.base[oc % 2] + 16 * NMP * (oc // 2 + 1))
                    ps = PSA if i % 2 == 0 else PSB
                    for hc in range(NMP):
                        hw = min(P, HID - P * hc)
                        ins = tensor.matmul(ps[:], w2t[oc % 2][0:hw, P * hc:P * (hc + 1)],
                                            hg[hc][0:hw, 512 * tb:512 * (tb + 1)],
                                            start=(hc == 0), stop=(hc == NMP - 1))
                    s_mm.inc(ins)
                em.on("tensor", f3_mm)

                def f3_out(vector, oc=oc, tb=tb):
                    i = 4 * oc + tb
                    s_mm.wait(vector)
                    ldxs.wait(vector, i)
                    if i >= 2:
                        s_out.wait(vector)
                    ps = PSA if i % 2 == 0 else PSB
                    s_ev.inc(vector.scalar_tensor_tensor(osb[i % 2][:], ps[:],
                                                         b2col[:, oc:oc + 1],
                                                         xsl[i % 2][:],
                                                         ALU.add, ALU.add))
                em.on("vector", f3_out)

                def f3_st(gpsimd, oc=oc, tb=tb):
                    i = 4 * oc + tb
                    s_ev.wait(gpsimd)
                    gpsimd.dma_start(out=out_d[P * oc:P * (oc + 1), 512 * tb:512 * (tb + 1)],
                                     in_=osb[i % 2][:]).then_inc(s_out.h, 16)
                    s_out.n += 16
                em.on("gpsimd", f3_st)

        def fin(gpsimd):
            s_out.wait(gpsimd)
        em.on("gpsimd", fin)

    S.barrier()




def io_stage_c_ffn(nc, phases=3, pref="", ki="ExternalInput", ko="ExternalOutput"):
    HID = 2730
    OWN = 2048
    io = {}
    io["cf_x1"] = nc.dram_tensor(pref + "cf_x1", [D, 512 + OWN], F32, kind=ki).ap()
    io["cf_g"] = nc.dram_tensor(pref + "cf_g", [P, 8], F32, kind=ki).ap()
    io["cf_b"] = nc.dram_tensor(pref + "cf_b", [P, 8], F32, kind=ki).ap()
    io["cf_w1"] = nc.dram_tensor(pref + "cf_w1", [D, 2 * HID], F32, kind=ki).ap()
    io["cf_b1"] = nc.dram_tensor(pref + "cf_b1", [1, 2 * HID], F32, kind=ki).ap()
    io["cf_w2"] = nc.dram_tensor(pref + "cf_w2", [HID, D], F32, kind=ki).ap()
    io["cf_b2"] = nc.dram_tensor(pref + "cf_b2", [P, 8], F32, kind=ki).ap()
    io["cc_cols_f32"] = nc.dram_tensor("cc_cols_f32", [P, 4], F32, kind=ki).ap()
    io["cc_row_f32"] = nc.dram_tensor("cc_row_f32", [1, 768], F32, kind=ki).ap()
    io["cf_out"] = nc.dram_tensor(pref + "cf_out", [D, OWN], F32, kind=ko).ap()
    return io


def build_stage_c_ffn(phases=3):
    nc = bass.Bass("TRN2", target_bir_lowering=False, debug=False)
    io = io_stage_c_ffn(nc, phases=phases)
    es = ExitStack()
    with es, nc.Block() as block:
        S = Shared(nc, block, es)
        emit_stage_c_ffn(S, io, phases=phases)
    return nc


def emit_stage_d(S, io):
    NT = 2048 // P

    nc = S.nc
    block = S.block
    x_d = io["d_x"]
    z_d = io["d_z"]
    ffn_d = io["d_ffn"]
    wp_d = io["d_wproj"]
    bp_d = io["d_bprojrow"]
    idf_d = io["cc_ident_f32"]
    rowf_d = io["cc_row_f32"]
    x1_d = io["d_x1"]
    xn1_d = io["d_xn1"]
    cx = StageCtx(S)
    z_sb = cx.sb("z_sb", [P, 8 * 2048], BF16)
    ffn_sb = cx.sb("ffn_sb", [P, 8 * 2048], F32)
    wp_sb = cx.sb("wp_sb", [P, 8 * D], BF16)
    bprow = cx.sb("bprow", [1, D], F32)
    idf = cx.sb("idf", [P, P], F32)
    rowf = cx.sb("rowf", [1, 768], F32)
    xt = [cx.sb(f"xt{p}", [P, D], F32) for p in range(2)]
    xa = [cx.sb(f"xa{p}", [P, D], F32) for p in range(2)]
    xsq = cx.sb("xsq", [P, D], F32)
    st = cx.sb("st", [P, 16], F32)
    xnb = [cx.sb(f"xnb{p}", [P, D], BF16) for p in range(2)]
    PSP = [cx.ps(f"PSP{p}", [P, 512]) for p in range(2)]
    PST = [cx.ps(f"PST{p}", [P, P]) for p in range(2)]

    ldc = cx.sem("ldc")
    ldg = cx.sem("ldg")
    ldx = DmaStream(cx, "ldx", 2)
    s_mm = cx.sem("s_mm")
    s_ea = cx.sem("s_ea")
    s_ev = cx.sem("s_ev")
    s_out = cx.sem("s_out")
    ea_b = s_ea.n

    with cx.es:
        em = S.em

        def p0(sync):
            for c in range(8):
                sync.dma_start(out=z_sb[:, 2048 * c:2048 * (c + 1)],
                               in_=z_d[P * c:P * (c + 1), :]).then_inc(ldc.h, 16)
                sync.dma_start(out=ffn_sb[:, 2048 * c:2048 * (c + 1)],
                               in_=ffn_d[P * c:P * (c + 1), :]).then_inc(ldc.h, 16)
            sync.dma_start(out=bprow[:], in_=bp_d[:, :]).then_inc(ldc.h, 16)
            sync.dma_start(out=idf[:], in_=idf_d[:, :]).then_inc(ldc.h, 16)
            sync.dma_start(out=rowf[:], in_=rowf_d[:, :]).then_inc(ldc.h, 16)
            ldc.n += 19 * 16
        em.on("sync", p0)

        def p0g(gpsimd):
            for kc in range(8):
                gpsimd.dma_start(out=wp_sb[:, D * kc:D * (kc + 1)],
                                 in_=wp_d[P * kc:P * (kc + 1), :]).then_inc(ldg.h, 16)
            ldg.n += 8 * 16
        em.on("gpsimd", p0g)

        for i in range(NT):
            def d_ldx(sync, i=i):
                if i >= 2:
                    s_ev.wait(sync, s_ev.n - 6)
                ldx.issue(sync, i, xt[i % 2][:], x_d[P * i:P * (i + 1), :])
            em.on("sync", d_ldx)

            for c in range(8):
                def d_tr(tensor, i=i, c=c):
                    if i == 0 and c == 0:
                        ldc.wait(tensor)
                        ldg.wait(tensor)
                    if not (i == 0 and c < 2):
                        s_ev.wait(tensor, s_ev.n - 1)
                    s_mm.inc(tensor.transpose(PST[c % 2][:, :],
                                              ffn_sb[:, 2048 * c + P * i:2048 * c + P * (i + 1)],
                                              idf[:]))
                em.on("tensor", d_tr)

                def d_acc(vector, i=i, c=c):
                    s_mm.wait(vector)
                    if c == 0:
                        ldx.wait(vector, i)
                        if i >= 2:
                            s_out.wait(vector)
                    s_ev.inc(vector.tensor_add(xa[i % 2][:, P * c:P * (c + 1)],
                                               xt[i % 2][:, P * c:P * (c + 1)],
                                               PST[c % 2][:, :]))
                em.on("vector", d_acc)

            for cb in range(2):
                def d_mm(tensor, i=i, cb=cb):
                    s_ev.wait(tensor, s_ev.n)
                    ps = PSP[cb]
                    tensor.matmul(ps[:], rowf[0:1, 0:128], bprow[0:1, 512 * cb:512 * (cb + 1)],
                                  start=True, stop=False)
                    for fc in range(8):
                        ins = tensor.matmul(ps[:],
                                            z_sb[:, 2048 * fc + P * i:2048 * fc + P * (i + 1)],
                                            wp_sb[:, D * fc + 512 * cb:D * fc + 512 * (cb + 1)],
                                            start=False, stop=(fc == 7))
                    s_mm.inc(ins)
                em.on("tensor", d_mm)

            def d_add(vector, i=i):
                s_mm.wait(vector)
                vector.drain()
                vector.tensor_add(xa[i % 2][:, 0:512], xa[i % 2][:, 0:512], PSP[0][:])
                s_ev.inc(vector.tensor_add(xa[i % 2][:, 512:1024], xa[i % 2][:, 512:1024],
                                           PSP[1][:]))
                vector.drain()
            em.on("vector", d_add)

            def d_stats(vector, i=i):
                o = 8 * (i % 2)
                if i >= 2:
                    vector.wait_ge(s_ea.h, ea_b + i - 1)
                vector.drain()
                vector.tensor_mul(xsq[:], xa[i % 2][:], xa[i % 2][:])
                vector.drain()
                vector.reduce_sum(st[:, o + 1:o + 2], xsq[:], axis=AX.X)
                vector.drain()
                vector.reduce_sum(st[:, o + 0:o + 1], xa[i % 2][:], axis=AX.X)
                vector.drain()
                vector.tensor_scalar_mul(st[:, o + 2:o + 3], st[:, o + 0:o + 1], -1.0 / D)
                vector.drain()
                vector.tensor_mul(st[:, o + 3:o + 4], st[:, o + 2:o + 3], st[:, o + 2:o + 3])
                vector.drain()
                vector.scalar_tensor_tensor(st[:, o + 3:o + 4], st[:, o + 1:o + 2], 1.0 / D,
                                            st[:, o + 3:o + 4], ALU.mult, ALU.subtract)
                vector.drain()
                s_ev.inc(vector.tensor_scalar_add(st[:, o + 3:o + 4], st[:, o + 3:o + 4], 1e-5))
            em.on("vector", d_stats)

            def d_rstd(scalar, i=i):
                o = 8 * (i % 2)
                s_ev.wait(scalar)
                s_ea.inc(scalar.activation(out=st[:, o + 4:o + 5], in_=st[:, o + 3:o + 4],
                                           func=AF.Sqrt))
            em.on("scalar", d_rstd)

            def d_ap(vector, i=i):
                o = 8 * (i % 2)
                s_ea.wait(vector)
                recip_approx(vector, st[:, o + 5:o + 6], st[:, o + 4:o + 5],
                             st[:, o + 6:o + 7])
                if i >= 2:
                    s_out.wait(vector)
                s_ev.inc(vector.tensor_scalar(xnb[i % 2][:], xa[i % 2][:], st[:, o + 2:o + 3],
                                              st[:, o + 5:o + 6], ALU.add, ALU.mult))
            em.on("vector", d_ap)

            def d_st(gpsimd, i=i):
                s_ev.wait(gpsimd)
                gpsimd.dma_start(out=x1_d[P * i:P * (i + 1), :], in_=xa[i % 2][:]).then_inc(s_out.h, 16)
                gpsimd.dma_start(out=xn1_d[P * i:P * (i + 1), :], in_=xnb[i % 2][:]).then_inc(s_out.h, 16)
                s_out.n += 32
            em.on("gpsimd", d_st)

        def fin(gpsimd):
            s_out.wait(gpsimd)
        em.on("gpsimd", fin)

    S.barrier()




def io_stage_d(nc, pref="", ki="ExternalInput", ko="ExternalOutput"):
    io = {}
    io["d_x"] = nc.dram_tensor(pref + "d_x", [2048, D], F32, kind=ki).ap()
    io["d_z"] = nc.dram_tensor(pref + "d_z", [D, 2048], BF16, kind=ki).ap()
    io["d_ffn"] = nc.dram_tensor(pref + "d_ffn", [D, 2048], F32, kind=ki).ap()
    io["d_wproj"] = nc.dram_tensor(pref + "d_wproj", [D, D], F32, kind=ki).ap()
    io["d_bprojrow"] = nc.dram_tensor(pref + "d_bprojrow", [1, D], F32, kind=ki).ap()
    io["cc_ident_f32"] = nc.dram_tensor("cc_ident_f32", [P, P], F32, kind=ki).ap()
    io["cc_row_f32"] = nc.dram_tensor("cc_row_f32", [1, 768], F32, kind=ki).ap()
    io["d_x1"] = nc.dram_tensor(pref + "d_x1", [2048, D], F32, kind=ko).ap()
    io["d_xn1"] = nc.dram_tensor(pref + "d_xn1", [2048, D], BF16, kind=ko).ap()
    return io


def build_stage_d():
    nc = bass.Bass("TRN2", target_bir_lowering=False, debug=False)
    io = io_stage_d(nc)
    es = ExitStack()
    with es, nc.Block() as block:
        S = Shared(nc, block, es)
        emit_stage_d(S, io)
    return nc


C1G = 0.7978845608028654
C2G = C1G * 0.044715


def emit_stage_f(S, io):
    NT = 2048 // P
    NFC = 32

    nc = S.nc
    block = S.block
    x1_d = io["f_x1"]
    z_d = io["f_z"]
    wp_d = io["f_wproj"]
    xn1_d = io["f_xn1"]
    u1_d = io["f_u1"]
    v1_d = io["f_v1"]
    b1_d = io["f_b1row"]
    u2_d = io["f_u2"]
    v2_d = io["f_v2"]
    b2_d = io["f_b2row"]
    id_d = io["cc_ident"]
    rowf_d = io["cc_row_f32"]
    out_d = io["f_out"]
    cx = StageCtx(S)
    z_sb = cx.sb("z_sb", [P, 8 * 2048], BF16)
    xn_sb = cx.sb("xn_sb", [P, 2 * D], BF16)
    xnfm = cx.sb("xnfm", [P, 8 * 2048], BF16)
    wp_sb = cx.sb("wp_sb", [P, 8 * D], BF16)
    u1_sb = cx.sb("u1_sb", [P, 8 * 256], BF16)
    v1_sb = cx.sb("v1_sb", [P, 2 * 4096], BF16)
    u2_sb = cx.sb("u2_sb", [P, NFC * 256], BF16)
    v2_sb = cx.sb("v2_sb", [P, 2 * D], BF16)
    b1col = cx.sb("b1col", [P, NFC], F32)
    b2row = cx.sb("b2row", [1, D], F32)
    ident = cx.sb("ident", [P, P], BF16)
    rowf = cx.sb("rowf", [1, 768], F32)
    p_sb = cx.sb("p_sb", [P, 2 * 2048], BF16)
    q_sb = cx.sb("q_sb", [P, 2 * 512], BF16)
    hbf = cx.sb("hbf", [P, 1024], BF16)
    xsl = [cx.sb(f"xsl{p}", [P, D], F32) for p in range(2)]
    osb = [cx.sb(f"osb{p}", [P, D], F32) for p in range(2)]
    PSH = [cx.ps(f"PSH{p}", [P, 512]) for p in range(2)]
    PSQ = [cx.ps(f"PSQ{p}", [P, 512]) for p in range(2)]
    PSO = [cx.ps(f"PSO{p}", [P, 512]) for p in range(2)]
    PSX = cx.ps("PSX", [P, 512], BF16)

    ldc = cx.sem("ldc")
    ldg = cx.sem("ldg")
    ldx = DmaStream(cx, "ldx", 2)
    ldn = DmaStream(cx, "ldn", 2)
    s_mm = cx.sem("s_mm")
    s_ea = cx.sem("s_ea")
    s_ev = cx.sem("s_ev")
    s_out = cx.sem("s_out")

    with cx.es:
        em = S.em

        def p0(sync):
            for c in range(8):
                sync.dma_start(out=z_sb[:, 2048 * c:2048 * (c + 1)],
                               in_=z_d[P * c:P * (c + 1), :]).then_inc(ldc.h, 16)
            for n, (dst, srcd) in enumerate(((b2row, b2_d), (ident, id_d),
                                             (rowf, rowf_d))):
                sync.dma_start(out=dst[:], in_=srcd[:, :]).then_inc(ldc.h, 16)
            ldc.n += 11 * 16
            for fc in range(NFC):
                sync.dma_start(out=b1col[:, fc:fc + 1],
                               in_=b1_d[0:1, P * fc:P * (fc + 1)].rearrange("o p -> p o")
                               ).then_inc(ldc.h, 16)
                ldc.n += 16
        em.on("sync", p0)

        def p0g(gpsimd):
            for kc in range(8):
                gpsimd.dma_start(out=wp_sb[:, D * kc:D * (kc + 1)],
                                 in_=wp_d[P * kc:P * (kc + 1), :]).then_inc(ldg.h, 16)
                gpsimd.dma_start(out=u1_sb[:, 256 * kc:256 * (kc + 1)],
                                 in_=u1_d[P * kc:P * (kc + 1), :]).then_inc(ldg.h, 16)
            for rc in range(2):
                gpsimd.dma_start(out=v1_sb[:, 4096 * rc:4096 * (rc + 1)],
                                 in_=v1_d[P * rc:P * (rc + 1), :]).then_inc(ldg.h, 16)
                gpsimd.dma_start(out=v2_sb[:, D * rc:D * (rc + 1)],
                                 in_=v2_d[P * rc:P * (rc + 1), :]).then_inc(ldg.h, 16)
            for fc in range(NFC):
                gpsimd.dma_start(out=u2_sb[:, 256 * fc:256 * (fc + 1)],
                                 in_=u2_d[P * fc:P * (fc + 1), :]).then_inc(ldg.h, 16)
            ldg.n += (16 + 4 + NFC) * 16
        em.on("gpsimd", p0g)

        for i in range(NT):
            def t_ld(sync, i=i):
                if i >= 2:
                    s_mm.wait(sync)
                ldn.issue(sync, i, xn_sb[:, D * (i % 2):D * (i % 2 + 1)],
                          xn1_d[P * i:P * (i + 1), :])
            em.on("sync", t_ld)

            for g in range(2):
                def t_tr(tensor, i=i, g=g):
                    if g == 0:
                        ldn.wait(tensor, i)
                        if i == 0:
                            ldc.wait(tensor)
                    s_ea.wait(tensor)
                    o = D * (i % 2)
                    for c in range(4):
                        s_mm.inc(tensor.transpose(PSX[:, P * c:P * (c + 1)],
                                                  xn_sb[:, o + P * (4 * g + c):o + P * (4 * g + c + 1)],
                                                  ident[:]))
                em.on("tensor", t_tr)

                def t_ev(vector, i=i, g=g):
                    s_mm.wait(vector)
                    s_ea.inc(vector.tensor_copy(
                        xnfm.ap().rearrange("p (c t) -> p c t", c=8)[:, 4 * g:4 * g + 4,
                                                                    P * i:P * (i + 1)],
                        PSX.ap().rearrange("p (c t) -> p c t", c=4)[:, :, :]))
                em.on("vector", t_ev)

        for tb in range(4):
            def p_mm(tensor, tb=tb):
                if tb == 0:
                    s_ea.wait(tensor)
                    ldg.wait(tensor)
                if tb >= 2:
                    s_ea.wait(tensor, s_ea.n - 1)
                for rc in range(2):
                    ps = PSH[tb % 2] if rc == 0 else PSQ[tb % 2]
                    for kc in range(8):
                        ins = tensor.matmul(ps[:],
                                            u1_sb[:, 256 * kc + 128 * rc:256 * kc + 128 * (rc + 1)],
                                            xnfm[:, 2048 * kc + 512 * tb:2048 * kc + 512 * (tb + 1)],
                                            start=(kc == 0), stop=(kc == 7))
                s_mm.inc(ins)
            em.on("tensor", p_mm)

            def p_ev(vector, tb=tb):
                s_mm.wait(vector)
                vector.tensor_copy(p_sb[:, 512 * tb:512 * (tb + 1)], PSH[tb % 2][:])
                s_ea.inc(vector.tensor_copy(
                    p_sb[:, 2048 + 512 * tb:2048 + 512 * (tb + 1)], PSQ[tb % 2][:]))
            em.on("vector", p_ev)

        for tb in range(4):
            mm0f, ea0f = s_mm.n, s_ea.n
            MF = {}
            mmf, eaf = mm0f, ea0f
            for fc in range(NFC):
                mmf += 1; MF["h", fc] = mmf
                eaf += 1; MF["g", fc] = eaf
                mmf += 1; MF["q", fc] = mmf

            def hq_tensor(tensor, tb=tb, mm0f=mm0f, ea0f=ea0f, MF=MF):
                for fc in range(NFC):
                    if tb == 0 and fc == 0:
                        s_ea.wait(tensor, ea0f)
                        ldc.wait(tensor)
                    if not (tb == 0 and fc < 2):
                        tensor.wait_ge(s_ea.h, MF["g", fc - 2] if fc >= 2 else ea0f - 1)
                    ps = PSH[fc % 2]
                    for rc in range(2):
                        ins = tensor.matmul(ps[:],
                                            v1_sb[:, 4096 * rc + P * fc:4096 * rc + P * (fc + 1)],
                                            p_sb[:, 2048 * rc + 512 * tb:2048 * rc + 512 * (tb + 1)],
                                            start=(rc == 0), stop=(rc == 1))
                    ins.then_inc(s_mm.h, 1)
                    tensor.wait_ge(s_ea.h, MF["g", fc])
                    for rc in range(2):
                        ins = tensor.matmul(PSQ[rc][:],
                                            u2_sb[:, 256 * fc + 128 * rc:256 * fc + 128 * (rc + 1)],
                                            hbf[:, 512 * (fc % 2):512 * (fc % 2 + 1)],
                                            start=(fc == 0), stop=(fc == NFC - 1))
                    ins.then_inc(s_mm.h, 1)
            em.on("tensor", hq_tensor)

            def hq_scalar(scalar, tb=tb, mm0f=mm0f, MF=MF):
                for fc in range(NFC):
                    if fc >= 2:
                        scalar.wait_ge(s_mm.h, MF["q", fc - 2])
                    scalar.wait_ge(s_mm.h, MF["h", fc])
                    scalar.activation(out=hbf[:, 512 * (fc % 2):512 * (fc % 2 + 1)],
                                      in_=PSH[fc % 2][:], func=AF.Gelu_apprx_tanh,
                                      bias=b1col[:, fc:fc + 1]).then_inc(s_ea.h, 1)
            em.on("scalar", hq_scalar)
            s_mm.n, s_ea.n = mmf, eaf

            def q_ev(vector, tb=tb):
                s_mm.wait(vector)
                vector.tensor_copy(q_sb[:, 0:512], PSQ[0][:])
                s_ea.inc(vector.tensor_copy(q_sb[:, 512:1024], PSQ[1][:]))
            em.on("vector", q_ev)
            qev_mark = s_ea.n

            for jj in range(4):
                ii = 4 * tb + jj

                def f_ldx(sync, tb=tb, jj=jj, ii=ii):
                    if ii >= 2:
                        s_ev.wait(sync, s_ev.n - 1)
                    ldx.issue(sync, ii, xsl[ii % 2][:, :],
                              x1_d[P * ii:P * (ii + 1), :])
                em.on("sync", f_ldx)

                def o_mm(tensor, tb=tb, jj=jj, ii=ii, qev_mark=qev_mark):
                    tensor.wait_ge(s_ea.h, qev_mark)
                    if ii >= 1:
                        s_ev.wait(tensor, s_ev.n)
                    t0 = P * jj
                    for cb in range(2):
                        ps = PSO[cb]
                        tensor.matmul(ps[:], rowf[0:1, 0:128],
                                      b2row[0:1, 512 * cb:512 * (cb + 1)],
                                      start=True, stop=False)
                        for rc in range(2):
                            tensor.matmul(ps[:],
                                          q_sb[:, 512 * rc + t0:512 * rc + t0 + P],
                                          v2_sb[:, D * rc + 512 * cb:D * rc + 512 * (cb + 1)],
                                          start=False, stop=False)
                        for fcc in range(8):
                            ins = tensor.matmul(
                                ps[:],
                                z_sb[:, 2048 * fcc + 512 * tb + t0:2048 * fcc + 512 * tb + t0 + P],
                                wp_sb[:, D * fcc + 512 * cb:D * fcc + 512 * (cb + 1)],
                                start=False, stop=(fcc == 7))
                        s_mm.inc(ins)
                em.on("tensor", o_mm)

                def o_add(vector, tb=tb, jj=jj, ii=ii):
                    s_mm.wait(vector)
                    ldx.wait(vector, ii)
                    if ii >= 2:
                        s_out.wait(vector)
                    vector.tensor_add(osb[ii % 2][:, 0:512], xsl[ii % 2][:, 0:512],
                                      PSO[0][:])
                    s_ev.inc(vector.tensor_add(osb[ii % 2][:, 512:1024],
                                               xsl[ii % 2][:, 512:1024], PSO[1][:]))
                em.on("vector", o_add)

                def o_st(gpsimd, tb=tb, ii=ii):
                    s_ev.wait(gpsimd)
                    gpsimd.dma_start(out=out_d[P * ii:P * (ii + 1), :],
                                     in_=osb[ii % 2][:]).then_inc(s_out.h, 16)
                    s_out.n += 16
                em.on("gpsimd", o_st)

        def fin(gpsimd):
            s_out.wait(gpsimd)
        em.on("gpsimd", fin)
    S.barrier()





def io_stage_f(nc, pref="", ki="ExternalInput", ko="ExternalOutput"):
    io = {}
    io["f_x1"] = nc.dram_tensor(pref + "f_x1", [2048, D], F32, kind=ki).ap()
    io["f_z"] = nc.dram_tensor(pref + "f_z", [D, 2048], BF16, kind=ki).ap()
    io["f_wproj"] = nc.dram_tensor(pref + "f_wproj", [D, D], F32, kind=ki).ap()
    io["f_xn1"] = nc.dram_tensor(pref + "f_xn1", [2048, D], BF16, kind=ki).ap()
    io["f_u1"] = nc.dram_tensor(pref + "f_u1", [D, 256], F32, kind=ki).ap()
    io["f_v1"] = nc.dram_tensor(pref + "f_v1", [256, 4096], F32, kind=ki).ap()
    io["f_b1row"] = nc.dram_tensor(pref + "f_b1row", [1, 4096], F32, kind=ki).ap()
    io["f_u2"] = nc.dram_tensor(pref + "f_u2", [4096, 256], F32, kind=ki).ap()
    io["f_v2"] = nc.dram_tensor(pref + "f_v2", [256, D], F32, kind=ki).ap()
    io["f_b2row"] = nc.dram_tensor(pref + "f_b2row", [1, D], F32, kind=ki).ap()
    io["cc_ident"] = nc.dram_tensor("cc_ident", [P, P], BF16, kind=ki).ap()
    io["cc_row_f32"] = nc.dram_tensor("cc_row_f32", [1, 768], F32, kind=ki).ap()
    io["f_out"] = nc.dram_tensor(pref + "f_out", [2048, D], F32, kind=ko).ap()
    return io


def build_stage_f():
    nc = bass.Bass("TRN2", target_bir_lowering=False, debug=False)
    io = io_stage_f(nc)
    es = ExitStack()
    with es, nc.Block() as block:
        S = Shared(nc, block, es)
        emit_stage_f(S, io)
    return nc


def e_consts():
    c = {}
    s = np.arange(P)[:, None]
    t = np.arange(512)[None, :]
    m = np.concatenate([(t > 128 * p + s).astype(np.float32) for p in range(4)], axis=1)
    c["cc_maskS"] = m.astype(BF16NP)
    return c


def emit_stage_e(S, io, T=T):

    nc = S.nc
    block = S.block
    xn_d = io["e_xnfm"]
    v_d = io["e_v"]
    vfm_d = io["e_vfm"]
    w_d = {n: io[f"e_w{n}"] for n in ("q", "k", "k2")}
    mask_d = io["cc_mask"]
    maskS_d = io["cc_maskS"]
    colsb_d = io["cc_cols_bf"]
    rowf_d = io["cc_row_f32"]
    ident_d = io["cc_ident"]
    z_d = io["e_z"]
    cx = StageCtx(S)
    NTB = T // 512
    NKC = T // P
    xn = cx.sb("xn", [P, 8 * T], BF16)
    ws = {n: cx.sb(f"w{n}s", [P, 8 * 512], BF16) for n in ("q", "k", "k2")}
    mask = cx.sb("mask", [P, 2048], BF16)
    maskS = cx.sb("maskS", [P, 2048], BF16)
    colsb = cx.sb("colsb", [P, 4], BF16)
    rowf = cx.sb("rowf", [1, 768], F32)
    ident = cx.sb("ident", [P, P], BF16)
    v_sb = cx.sb("v_sb", [P, NKC * 512], BF16)
    vfm = cx.sb("vfm", [P, T], BF16)
    q_sb = cx.sb("q_sb", [P, T], BF16)
    k_sb = cx.sb("k_sb", [P, T], BF16)
    ka_sb = cx.sb("ka_sb", [P, T], BF16)
    qa_sb = cx.sb("qa_sb", [P, 512], BF16)
    qm_sb = cx.sb("qm_sb", [P, 512], BF16)
    e_sb = cx.sb("e_sb", [P, T], BF16)
    z_sb = cx.sb("z_sb", [P, T], BF16)
    wt_sb = cx.sb("wt_sb", [P, 2048], BF16)
    ebf = cx.sb("ebf", [P, 512], BF16)
    sqs = cx.sb("sqs", [P, 512], BF16)
    mx = cx.sb("mx", [1, 64], F32)
    pt = [cx.sb(f"pt{i}", [P, 1024], BF16) for i in range(2)]
    rsum = cx.sb("rsum", [1, 1024], F32)
    ysc = cx.sb("ysc", [P, 512], F32)
    ysc2 = cx.sb("ysc2", [P, 512], F32)
    rr = cx.sb("rr", [P, 512], F32)
    bias16 = cx.sb("bias16", [P, 16], F32)
    psA = [cx.ps(f"psA{i}", [P, 512]) for i in range(2)]
    psB = [cx.ps(f"psB{i}", [P, 512]) for i in range(2)]
    psY = cx.ps("psY", [P, 512])
    psS = cx.ps("psS", [P, 512])
    psM = cx.ps("psM", [P, 512])
    TRP = cx.ps("TRP", [P, 512], BF16)

    ldw = cx.sem("ldw")
    ldc = cx.sem("ldc")
    ldx = cx.sem("ldx")
    ldv = cx.sem("ldv")
    ldvf = cx.sem("ldvf")
    s_mm = cx.sem("s_mm")
    s_ea = cx.sem("s_ea")
    s_ev = cx.sem("s_ev")
    s_sc = cx.sem("s_sc")
    s_av = cx.sem("s_av")
    s_out = cx.sem("s_out")

    with cx.es:
        em = S.em

        def p0_sync(sync):
            for c in range(8):
                sync.dma_start(out=xn[:, T * c:T * (c + 1)],
                               in_=xn_d[P * c:P * (c + 1), :]).then_inc(ldx.h, 16)
            ldx.n += 8 * 16
            for tt in range(NKC):
                sync.dma_start(out=v_sb[:, 512 * tt:512 * (tt + 1)],
                               in_=v_d[P * tt:P * (tt + 1), :]).then_inc(ldv.h, 16)
            ldv.n += NKC * 16
            for dst, srcd in ((mask, mask_d), (maskS, maskS_d), (colsb, colsb_d),
                              (rowf, rowf_d), (ident, ident_d)):
                sync.dma_start(out=dst[:], in_=srcd[:, :]).then_inc(ldc.h, 16)
            ldc.n += 5 * 16
        em.on("sync", p0_sync)

        def p0_gp(gpsimd):
            for n in ("q", "k", "k2"):
                for kc in range(8):
                    gpsimd.dma_start(out=ws[n][:, 512 * kc:512 * (kc + 1)],
                                     in_=w_d[n][P * kc:P * (kc + 1), :]).then_inc(ldw.h, 16)
            ldw.n += 24 * 16
        em.on("gpsimd", p0_gp)

        for hp in range(4):
            def e_vfmld(gpsimd, hp=hp):
                if hp > 0:
                    s_ev.wait(gpsimd)
                gpsimd.dma_start(out=vfm[:], in_=vfm_d[P * hp:P * (hp + 1), :]).then_inc(ldvf.h, 16)
                ldvf.n += 16
            em.on("gpsimd", e_vfmld)
            vfm_mark = ldvf.n

            mm_b1, ea_b1 = s_mm.n, s_ea.n
            for which, wname in ((0, "q"), (1, "k"), (2, "k2")):
                for tb in range(NTB):
                    def e1_pe(tensor, which=which, wname=wname, tb=tb, hp=hp):
                        if hp == 0 and which == 0 and tb == 0:
                            ldx.wait(tensor)
                            ldw.wait(tensor)
                            ldc.wait(tensor)
                        if hp > 0 and which == 0 and tb == 0:
                            s_sc.wait(tensor)
                            s_av.wait(tensor)
                        idx = which * NTB + tb
                        if idx >= 2:
                            tensor.wait_ge(s_ea.h, ea_b1 + idx - 1)
                        ps = psA[idx % 2] if which != 1 else psB[idx % 2]
                        for kc in range(8):
                            ins = tensor.matmul(
                                ps[:], ws[wname][:, 512 * kc + 128 * hp:512 * kc + 128 * (hp + 1)],
                                xn[:, T * kc + 512 * tb:T * kc + 512 * (tb + 1)],
                                start=(kc == 0), stop=(kc == 7))
                        s_mm.inc(ins)
                    em.on("tensor", e1_pe)

                    if which != 2:
                        def e1_ev(vector, which=which, tb=tb):
                            idx = which * NTB + tb
                            vector.wait_ge(s_mm.h, mm_b1 + idx + 1)
                            ps = psA[idx % 2] if which != 1 else psB[idx % 2]
                            if which == 0:
                                vector.tensor_scalar_mul(
                                    q_sb[:, 512 * tb:512 * (tb + 1)], ps[:],
                                    0.125).then_inc(s_ea.h, 1)
                            else:
                                vector.tensor_copy(
                                    k_sb[:, 512 * tb:512 * (tb + 1)],
                                    ps[:]).then_inc(s_ea.h, 1)
                            s_ea.n += 1
                        em.on("vector", e1_ev)

            def e1_sig(scalar, mm_b1=mm_b1):
                for tb in range(NTB):
                    idx = 2 * NTB + tb
                    scalar.wait_ge(s_mm.h, mm_b1 + idx + 1)
                    scalar.activation(out=ka_sb[:, 512 * tb:512 * (tb + 1)],
                                      in_=psA[idx % 2][:], func=AF.Sigmoid,
                                      scale=0.0025).then_inc(s_ea.h, 1)
            em.on("scalar", e1_sig)
            s_ea.n += NTB

            for which, src_sb in ((0, q_sb), (1, k_sb)):
                for tb in range(NTB):
                    def e2_sq(vector, src_sb=src_sb, tb=tb):
                        s_mm.wait(vector)
                        vector.drain()
                        s_ev.inc(vector.tensor_mul(sqs[:],
                                                   src_sb[:, 512 * tb:512 * (tb + 1)],
                                                   src_sb[:, 512 * tb:512 * (tb + 1)]))
                    em.on("vector", e2_sq)
                    sq_mark = s_ev.n

                    def e2_mm(tensor, sq_mark=sq_mark):
                        tensor.wait_ge(s_ev.h, sq_mark)
                        i1 = tensor.matmul(psM[0:1, 0:512], colsb[:, 1:2], sqs[:],
                                           start=True, stop=True, tile_position=(0, 0))
                        i2 = tensor.matmul(psM[32:33, 0:512], colsb[:, 2:3], sqs[:],
                                           start=True, stop=True, tile_position=(0, 32),
                                           skip_group_check=True)
                        s_mm.inc(i1)
                        s_mm.inc(i2)
                    em.on("tensor", e2_mm)
                    mm_mark = s_mm.n

                    def e2_max(vector, which=which, tb=tb, mm_mark=mm_mark):
                        vector.wait_ge(s_mm.h, mm_mark)
                        if which == 0:
                            vector.reduce_max(mx[0:1, tb:tb + 1], psM[0:1, 0:512],
                                              axis=AX.X)
                            vector.drain()
                            vector.reduce_max(mx[0:1, 8 + tb:9 + tb], psM[32:33, 0:512],
                                              axis=AX.X)
                            vector.drain()
                            s_ev.inc(vector.nop())
                        else:
                            vector.reduce_max(mx[0:1, 20:21], psM[0:1, 0:512], axis=AX.X)
                            vector.drain()
                            vector.reduce_max(mx[0:1, 21:22], psM[32:33, 0:512], axis=AX.X)
                            vector.drain()
                            if tb == 0:
                                ins = vector.tensor_copy(mx[0:1, 16:18], mx[0:1, 20:22])
                            else:
                                ins = vector.tensor_max(mx[0:1, 16:18], mx[0:1, 16:18],
                                                        mx[0:1, 20:22])
                            vector.drain()
                            s_ev.inc(ins)
                    em.on("vector", e2_max)

            def e2_prod(vector, hp=hp):
                for hh in range(2):
                    vector.tensor_scalar(mx.ap()[0:1, 32 + hh:32 + hh + 2 * NTB:2],
                                         mx[0:1, 8 * hh:8 * hh + NTB],
                                         mx[0:1, 16 + hh:17 + hh], 1.0,
                                         ALU.mult, ALU.mult)
                    vector.drain()
                s_ev.inc(vector.nop())
            em.on("vector", e2_prod)
            prod_mark = s_ev.n

            def e2_sqB(scalar, prod_mark=prod_mark):
                scalar.wait_ge(s_ev.h, prod_mark)
                s_ea.inc(scalar.activation(out=mx[0:1, 48:48 + 2 * NTB],
                                           in_=mx[0:1, 32:32 + 2 * NTB], func=AF.Sqrt))
            em.on("scalar", e2_sqB)
            sqB_mark = s_ea.n

            def e2_neg(vector, sqB_mark=sqB_mark):
                vector.wait_ge(s_ea.h, sqB_mark)
                vector.tensor_scalar_mul(mx[0:1, 32:32 + 2 * NTB],
                                         mx[0:1, 48:48 + 2 * NTB], -1.0)
                vector.drain()
                s_ev.inc(vector.nop())
            em.on("vector", e2_neg)
            negB_mark = s_ev.n

            def p3_b16(tensor):
                s_ev.wait(tensor)
                s_ea.wait(tensor)
                s_mm.inc(tensor.matmul(psM[:, 0:2 * NTB], rowf[0:1, 0:128],
                                       mx[0:1, 32:32 + 2 * NTB], start=True, stop=True))
            em.on("tensor", p3_b16)
            b16mm_mark = s_mm.n

            def p3_b16cp(vector, b16mm_mark=b16mm_mark):
                vector.wait_ge(s_mm.h, b16mm_mark)
                s_ea.inc(vector.tensor_copy(bias16[:, 0:2 * NTB], psM[:, 0:2 * NTB]))
            em.on("vector", p3_b16cp)
            bcp_hp = s_ea.n

            ea0, ev0, mm0, sc0, av0 = s_ea.n, s_ev.n, s_mm.n, s_sc.n, s_av.n
            out0 = s_out.n
            ldv_n = ldv.n
            M = {}
            ea, ev, mm, sc, av = ea0, ev0, mm0, sc0, av0
            for qb in range(NTB):
                nkc = 4 * (qb + 1)
                for kc in range(nkc):
                    sc += 2; M["sc1", qb, kc] = sc
                    ea += 1; M["exp", qb, kc] = ea
                    if kc >= 4 * qb:
                        ev += 1; M["msk1", qb, kc] = ev
                    av += 4; M["av", qb, kc] = av
                M["lastav", qb] = av
                ea += 1; M["yev", qb] = ea
                ev += 1; M["rcp", qb] = ev
                mm += 1; M["rb", qb] = mm
                ev += 1; M["yn", qb] = ev
                ev += 1; M["e", qb] = ev
                for j in range(4):
                    mm += 1; M["etr", qb, j] = mm
                    ea += 1; M["eev", qb, j] = ea
                ev += 1; M["qa", qb] = ev
                for sc_ in range(nkc):
                    sc += 2; M["sc2", qb, sc_] = sc
                    ea += 1; M["wev", qb, sc_] = ea
                    if sc_ >= 4 * qb:
                        ev += 1; M["msk2", qb, sc_] = ev
                    av += 2; M["y2", qb, sc_] = av
                M["lasty2", qb] = av
                ev += 1; M["z", qb] = ev

            def p4_tensor(tensor, hp=hp, ldv_n=ldv_n):
                def emit_sc1(qb, kc):
                    tensor.wait_ge(s_ea.h, M["exp", qb, kc - 2] if kc >= 2
                                   else (M["wev", qb - 1, 4 * qb - 1] if qb
                                         else bcp_hp))
                    tensor.matmul(psA[kc % 2][:], k_sb[0:64, P * kc:P * (kc + 1)],
                                  q_sb[0:64, 512 * qb:512 * (qb + 1)],
                                  start=True, stop=True,
                                  tile_position=(0, 0)).then_inc(s_sc.h, 1)
                    tensor.matmul(psB[kc % 2][:], k_sb[64:128, P * kc:P * (kc + 1)],
                                  q_sb[64:128, 512 * qb:512 * (qb + 1)],
                                  start=True, stop=True,
                                  tile_position=(64, 0)).then_inc(s_sc.h, 1)

                def emit_av(qb, kc, nkc):
                    tensor.wait_ge(s_ea.h, M["exp", qb, kc])
                    if ("msk1", qb, kc) in M:
                        tensor.wait_ge(s_ev.h, M["msk1", qb, kc])
                    if kc == 0:
                        tensor.wait_ge(s_ea.h, M["yev", qb - 1] if qb else ea0)
                        tensor.wait_ge(s_ev.h, M["rcp", qb - 1] if qb else ev0)
                        if qb == 0:
                            tensor.wait_ge(ldv.h, ldv_n)
                    st_, sp_ = kc == 0, kc == nkc - 1
                    c0 = 512 * kc + 64 * (2 * hp)
                    c1 = 512 * kc + 64 * (2 * hp + 1)
                    for ii in (
                        tensor.matmul(psY[0:64, :], v_sb[:, c0:c0 + 64],
                                      pt[kc % 2][:, 0:512], start=st_, stop=sp_,
                                      tile_position=(0, 0)),
                        tensor.matmul(psY[64:128, :], v_sb[:, c1:c1 + 64],
                                      pt[kc % 2][:, 512:1024], start=st_, stop=sp_,
                                      tile_position=(0, 64), skip_group_check=True),
                        tensor.matmul(psS[0:1, 0:512], colsb[:, 0:1],
                                      pt[kc % 2][:, 0:512], start=st_, stop=sp_,
                                      tile_position=(0, 0)),
                        tensor.matmul(psS[32:33, 0:512], colsb[:, 0:1],
                                      pt[kc % 2][:, 512:1024], start=st_, stop=sp_,
                                      tile_position=(0, 32), skip_group_check=True),
                    ):
                        ii.then_inc(s_av.h, 1)

                def emit_w(qb, sc_):
                    tensor.wait_ge(s_ev.h, M["qa", qb])
                    tensor.wait_ge(s_ea.h, M["wev", qb, sc_ - 2] if sc_ >= 2
                                   else M["eev", qb, 3])
                    tensor.matmul(psA[sc_ % 2][:], ka_sb[0:64, P * sc_:P * (sc_ + 1)],
                                  qa_sb[0:64, :], start=True, stop=True,
                                  tile_position=(0, 0)).then_inc(s_sc.h, 1)
                    tensor.matmul(psB[sc_ % 2][:], ka_sb[64:128, P * sc_:P * (sc_ + 1)],
                                  qa_sb[64:128, :], start=True, stop=True,
                                  tile_position=(64, 0)).then_inc(s_sc.h, 1)

                def emit_y2(qb, sc_, nkc):
                    tensor.wait_ge(s_ea.h, M["wev", qb, sc_])
                    if ("msk2", qb, sc_) in M:
                        tensor.wait_ge(s_ev.h, M["msk2", qb, sc_])
                    if sc_ == 0:
                        tensor.wait_ge(s_ev.h, M["qa", qb])
                    par = sc_ % 2
                    st_, sp_ = sc_ == 0, sc_ == nkc - 1
                    tensor.matmul(psY[0:64, :], e_sb[:, P * sc_:P * sc_ + 64],
                                  wt_sb[:, 1024 * par:1024 * par + 512],
                                  start=st_, stop=sp_,
                                  tile_position=(0, 0)).then_inc(s_av.h, 1)
                    tensor.matmul(psY[64:128, :], e_sb[:, P * sc_ + 64:P * (sc_ + 1)],
                                  wt_sb[:, 1024 * par + 512:1024 * par + 1024],
                                  start=st_, stop=sp_, tile_position=(0, 64),
                                  skip_group_check=True).then_inc(s_av.h, 1)

                for qb in range(NTB):
                    nkc = 4 * (qb + 1)
                    for kc in range(nkc):
                        emit_sc1(qb, kc)
                        if kc >= 2:
                            emit_av(qb, kc - 2, nkc)
                    emit_av(qb, nkc - 2, nkc)
                    emit_av(qb, nkc - 1, nkc)
                    tensor.wait_ge(s_ev.h, M["rcp", qb])
                    tensor.wait_ge(s_ea.h, M["yev", qb])
                    tensor.matmul(psM[:, 0:512], rowf[0:1, 512:640], rsum[0:1, 0:512],
                                  start=True, stop=False)
                    tensor.matmul(psM[:, 0:512], rowf[0:1, 640:768],
                                  rsum[0:1, 512:1024], start=False,
                                  stop=True).then_inc(s_mm.h, 1)
                    for j in range(4):
                        tensor.wait_ge(s_ev.h, M["e", qb])
                        tensor.wait_ge(s_ea.h, M["eev", qb, j - 1] if j
                                       else M["yev", qb])
                        tensor.transpose(TRP[:, 0:P], ebf[:, P * j:P * (j + 1)],
                                         ident[:]).then_inc(s_mm.h, 1)
                    for sc_ in range(nkc):
                        emit_w(qb, sc_)
                        if sc_ >= 2:
                            emit_y2(qb, sc_ - 2, nkc)
                    emit_y2(qb, nkc - 2, nkc)
                    emit_y2(qb, nkc - 1, nkc)
            em.on("tensor", p4_tensor)

            def p4_scalar(scalar):
                for qb in range(NTB):
                    nkc = 4 * (qb + 1)
                    for kc in range(nkc):
                        scalar.wait_ge(s_sc.h, M["sc1", qb, kc])
                        if qb == 0 and kc == 0:
                            scalar.wait_ge(s_ea.h, bcp_hp)
                        scalar.wait_ge(s_av.h, M["av", qb, kc - 2] if kc >= 2
                                       else (M["lasty2", qb - 1] if qb else av0))
                        scalar.activation(out=pt[kc % 2][:, 0:512], in_=psA[kc % 2][:],
                                          func=AF.Exp,
                                          bias=bias16[:, 2 * qb:2 * qb + 1])
                        scalar.activation(out=pt[kc % 2][:, 512:1024],
                                          in_=psB[kc % 2][:], func=AF.Exp,
                                          bias=bias16[:, 2 * qb + 1:2 * qb + 2]
                                          ).then_inc(s_ea.h, 1)
            em.on("scalar", p4_scalar)

            def p4_vector(vector, hp=hp, vfm_mark=vfm_mark):
                for qb in range(NTB):
                    nkc = 4 * (qb + 1)
                    for kc in range(4 * qb, nkc):
                        pat = kc - 4 * qb
                        vector.wait_ge(s_ea.h, M["exp", qb, kc])
                        vector.tensor_mul(pt[kc % 2][:, 0:512], pt[kc % 2][:, 0:512],
                                          mask[:, 512 * pat:512 * (pat + 1)])
                        vector.tensor_mul(pt[kc % 2][:, 512:1024],
                                          pt[kc % 2][:, 512:1024],
                                          mask[:, 512 * pat:512 * (pat + 1)]
                                          ).then_inc(s_ev.h, 1)
                    vector.wait_ge(s_av.h, M["lastav", qb])
                    vector.tensor_copy(ysc[:], psY[:]).then_inc(s_ea.h, 1)
                    vector.tensor_copy(rsum[0:1, 0:512], psS[0:1, 0:512])
                    vector.tensor_copy(rsum[0:1, 512:1024],
                                       psS[32:33, 0:512]).then_inc(s_ev.h, 1)
                    vector.wait_ge(s_mm.h, M["rb", qb])
                    vector.reciprocal(rr[:], psM[:, 0:512])
                    vector.drain()
                    vector.tensor_mul(ysc2[:], ysc[:], rr[:]).then_inc(s_ev.h, 1)
                    vector.drain()
                    w_ = 512 if qb < NTB - 1 else 511
                    vector.wait_ge(ldvf.h, vfm_mark)
                    ins = vector.tensor_sub(ebf[:, 0:w_],
                                            vfm[:, 512 * qb + 1:512 * qb + 1 + w_],
                                            ysc2[:, 0:w_])
                    if w_ == 511:
                        vector.drain()
                        ins = vector.memset(ebf[:, 511:512], 0.0)
                    ins.then_inc(s_ev.h, 1)
                    vector.drain()
                    for j in range(4):
                        vector.wait_ge(s_mm.h, M["etr", qb, j])
                        if qb == 0 and j == 0:
                            vector.wait_ge(s_av.h, av0)
                        vector.tensor_copy(e_sb[:, P * (4 * qb + j):P * (4 * qb + j + 1)],
                                           TRP[:, 0:P]).then_inc(s_ea.h, 1)
                    vector.wait_ge(s_sc.h, M["sc1", qb, nkc - 1])
                    vector.drain()
                    vector.tensor_scalar(qm_sb[:], q_sb[:, 512 * qb:512 * (qb + 1)],
                                         0.0, 0.98, ALU.min, ALU.mult)
                    vector.drain()
                    vector.scalar_tensor_tensor(
                        qa_sb[:], q_sb[:, 512 * qb:512 * (qb + 1)], 0.02, qm_sb[:],
                        ALU.mult, ALU.add).then_inc(s_ev.h, 1)
                    vector.drain()
                    for sc_ in range(nkc):
                        par = sc_ % 2
                        vector.wait_ge(s_sc.h, M["sc2", qb, sc_])
                        if sc_ >= 2:
                            vector.wait_ge(s_av.h, M["y2", qb, sc_ - 2])
                        vector.tensor_copy(wt_sb[:, 1024 * par:1024 * par + 512],
                                           psA[sc_ % 2][:])
                        vector.tensor_copy(wt_sb[:, 1024 * par + 512:1024 * par + 1024],
                                           psB[sc_ % 2][:]).then_inc(s_ea.h, 1)
                        if sc_ >= 4 * qb:
                            pat = sc_ - 4 * qb
                            vector.drain()
                            vector.tensor_mul(wt_sb[:, 1024 * par:1024 * par + 512],
                                              wt_sb[:, 1024 * par:1024 * par + 512],
                                              maskS[:, 512 * pat:512 * (pat + 1)])
                            vector.tensor_mul(
                                wt_sb[:, 1024 * par + 512:1024 * par + 1024],
                                wt_sb[:, 1024 * par + 512:1024 * par + 1024],
                                maskS[:, 512 * pat:512 * (pat + 1)]).then_inc(s_ev.h, 1)
                    vector.wait_ge(s_av.h, M["lasty2", qb])
                    if qb == 0:
                        vector.wait_ge(s_out.h, out0)
                    vector.tensor_add(z_sb[:, 512 * qb:512 * (qb + 1)], ysc2[:],
                                      psY[:]).then_inc(s_ev.h, 1)
            em.on("vector", p4_vector)
            s_ea.n, s_ev.n, s_mm.n, s_sc.n, s_av.n = ea, ev, mm, sc, av

            def pout(gpsimd, hp=hp):
                s_ev.wait(gpsimd)
                gpsimd.dma_start(out=z_d[P * hp:P * (hp + 1), :], in_=z_sb[:]).then_inc(s_out.h, 16)
                s_out.n += 16
                s_out.wait(gpsimd)
            em.on("gpsimd", pout)

    S.barrier()




def io_stage_e(nc, T=T, pref="", ki="ExternalInput", ko="ExternalOutput"):
    io = {}
    io["e_xnfm"] = nc.dram_tensor(pref + "e_xnfm", [D, T], BF16, kind=ki).ap()
    io["e_v"] = nc.dram_tensor(pref + "e_v", [T, 512], BF16, kind=ki).ap()
    io["e_vfm"] = nc.dram_tensor(pref + "e_vfm", [512, T], BF16, kind=ki).ap()
    for n in ("q", "k", "k2"):
        io[f"e_w{n}"] = nc.dram_tensor(f"e_w{n}", [D, 512], F32, kind=ki).ap()
    io["cc_mask"] = nc.dram_tensor("cc_mask", [P, 2048], BF16, kind=ki).ap()
    io["cc_maskS"] = nc.dram_tensor("cc_maskS", [P, 2048], BF16, kind=ki).ap()
    io["cc_cols_bf"] = nc.dram_tensor("cc_cols_bf", [P, 4], BF16, kind=ki).ap()
    io["cc_row_f32"] = nc.dram_tensor("cc_row_f32", [1, 768], F32, kind=ki).ap()
    io["cc_ident"] = nc.dram_tensor("cc_ident", [P, P], BF16, kind=ki).ap()
    io["e_z"] = nc.dram_tensor(pref + "e_z", [512, T], BF16, kind=ko).ap()
    return io


def build_stage_e(T=T):
    nc = bass.Bass("TRN2", target_bir_lowering=False, debug=False)
    io = io_stage_e(nc, T=T)
    es = ExitStack()
    with es, nc.Block() as block:
        S = Shared(nc, block, es)
        emit_stage_e(S, io, T=T)
    return nc


def _dram(nc, name, shape, dtype, kind="Internal"):
    if not hasattr(nc, "_dram_cache"):
        nc._dram_cache = {}
    if name not in nc._dram_cache:
        nc._dram_cache[name] = nc.dram_tensor(name, shape, dtype, kind=kind).ap()
    return nc._dram_cache[name]


def emit_copy(S, dst, src, rows, cast_via_sbuf=None):
    cx = StageCtx(S)
    sem = cx.sem("adp")
    if cast_via_sbuf is None:
        def cp(gpsimd):
            for r0 in range(0, rows, 512):
                r1 = min(rows, r0 + 512)
                gpsimd.dma_start(out=dst[r0:r1, :], in_=src[r0:r1, :]).then_inc(sem.h, 16)
                sem.n += 16
            sem.wait(gpsimd)
        S.em.on("gpsimd", cp)
    else:
        cols = dst.shape[1]
        t = cx.sb("cpt", [P, 2 * cols], cast_via_sbuf)
        with cx.es:
            def cp(gpsimd):
                for i in range(rows // P):
                    pp = i % 2
                    if i >= 2:
                        sem.wait(gpsimd, sem.n - 16)
                    gpsimd.dma_start(out=t[:, cols * pp:cols * (pp + 1)],
                                     in_=src[P * i:P * (i + 1), :]).then_inc(sem.h, 16)
                    sem.n += 16
                    gpsimd.wait_ge(sem.h, sem.n)
                    gpsimd.dma_start(out=dst[P * i:P * (i + 1), :],
                                     in_=t[:, cols * pp:cols * (pp + 1)]).then_inc(sem.h, 16)
                    sem.n += 16
                sem.wait(gpsimd)
            S.em.on("gpsimd", cp)
    S.barrier()


def emit_select(S, dst, a, b, rows, cols, sb_dtype, flags, only_b=False):
    cx = StageCtx(S)
    sem = cx.sem("adp")
    s_ev = cx.sem("s_ev")
    ta = cx.sb("sel_a", [P, 1024], sb_dtype)
    tb = cx.sb("sel_b", [P, 1024], sb_dtype)
    to = cx.sb("sel_o", [P, 1024], sb_dtype)
    fl = cx.sb("sel_f", [P, 2], F32)
    with cx.es:
        def ld0(sync):
            sync.dma_start(out=fl[:], in_=flags[:, :]).then_inc(sem.h, 16)
            sem.n += 16
        S.em.on("sync", ld0)
        n_t = 0
        for i in range(rows // P):
            for c0 in range(0, cols, 512):
                c1 = min(cols, c0 + 512)
                w = c1 - c0
                pp = n_t % 2
                def ld(gpsimd, i=i, c0=c0, c1=c1, w=w, pp=pp, n_t=n_t):
                    if n_t >= 2:
                        s_ev.wait(gpsimd, s_ev.n - 1)
                    if not only_b:
                        gpsimd.dma_start(out=ta[:, 512 * pp:512 * pp + w],
                                         in_=a[P * i:P * (i + 1), c0:c1]).then_inc(sem.h, 16)
                        sem.n += 16
                    gpsimd.dma_start(out=tb[:, 512 * pp:512 * pp + w],
                                     in_=b[P * i:P * (i + 1), c0:c1]).then_inc(sem.h, 16)
                    sem.n += 16
                S.em.on("gpsimd", ld)
                def work(vector, i=i, w=w, pp=pp):
                    sem.wait(vector)
                    if only_b:
                        vector.tensor_scalar_mul(to[:, 512 * pp:512 * pp + w],
                                                 tb[:, 512 * pp:512 * pp + w], fl[:, 1:2])
                    else:
                        vector.tensor_scalar_mul(to[:, 512 * pp:512 * pp + w],
                                                 ta[:, 512 * pp:512 * pp + w], fl[:, 0:1])
                        vector.drain()
                        vector.scalar_tensor_tensor(to[:, 512 * pp:512 * pp + w],
                                                    tb[:, 512 * pp:512 * pp + w],
                                                    fl[:, 1:2], to[:, 512 * pp:512 * pp + w],
                                                    ALU.mult, ALU.add)
                    vector.drain()
                    s_ev.inc(vector.nop())
                S.em.on("vector", work)
                def st(gpsimd, i=i, c0=c0, c1=c1, w=w, pp=pp):
                    s_ev.wait(gpsimd)
                    gpsimd.dma_start(out=dst[P * i:P * (i + 1), c0:c1],
                                     in_=to[:, 512 * pp:512 * pp + w]).then_inc(sem.h, 16)
                    sem.n += 16
                S.em.on("gpsimd", st)
                n_t += 1
        def fin(gpsimd):
            sem.wait(gpsimd)
        S.em.on("gpsimd", fin)
    S.barrier()


def emit_tm2fm(S, dst, src, N, D2, ident_ap):
    cx = StageCtx(S)
    sem = cx.sem("adp")
    s_mm = cx.sem("s_mm")
    s_ea = cx.sem("s_ea")
    s_out = cx.sem("s_out")
    NCH = D2 // P
    tile = cx.sb("t2f_t", [P, 2 * D2], BF16)
    fmb = cx.sb("t2f_f", [P, NCH * N], BF16)
    ident = cx.sb("t2f_i", [P, P], BF16)
    ps = cx.ps("t2f_p", [P, P], BF16)
    with cx.es:
        def ld0(sync):
            sync.dma_start(out=ident[:], in_=ident_ap[:, :]).then_inc(sem.h, 16)
            sem.n += 16
        S.em.on("sync", ld0)
        for i in range(N // P):
            def ld(sync, i=i):
                if i >= 2:
                    s_mm.wait(sync, s_mm.n - NCH)
                sync.dma_start(out=tile[:, D2 * (i % 2):D2 * (i % 2 + 1)],
                               in_=src[P * i:P * (i + 1), :]).then_inc(sem.h, 16)
                sem.n += 16
            S.em.on("sync", ld)
            for c in range(NCH):
                def tr(tensor, i=i, c=c):
                    if c == 0:
                        sem.wait(tensor)
                    s_ea.wait(tensor)
                    s_mm.inc(tensor.transpose(
                        ps[:, :], tile[:, D2 * (i % 2) + P * c:D2 * (i % 2) + P * (c + 1)],
                        ident[:]))
                S.em.on("tensor", tr)
                def ev(vector, i=i, c=c):
                    s_mm.wait(vector)
                    s_ea.inc(vector.tensor_copy(fmb[:, N * c + P * i:N * c + P * (i + 1)],
                                                ps[:, :]))
                S.em.on("vector", ev)
        def st(gpsimd):
            s_ea.wait(gpsimd)
            for c in range(NCH):
                gpsimd.dma_start(out=dst[P * c:P * (c + 1), :],
                                 in_=fmb[:, N * c:N * (c + 1)]).then_inc(s_out.h, 16)
                s_out.n += 16
            s_out.wait(gpsimd)
        S.em.on("gpsimd", st)
    S.barrier()


def emit_allgather(S, out_ap, in_ap):
    cx = StageCtx(S)
    ccs = cx.sem("ccs")
    def cc(gpsimd):
        gpsimd.collective_compute(
            "AllGather", ALU.bypass,
            replica_groups=[[0, 1], [2, 3], [4, 5], [6, 7]],
            ins=[in_ap.opt()], outs=[out_ap.opt()],
        ).then_inc(ccs.h, 1)
        ccs.n += 1
        ccs.wait(gpsimd)
    S.em.on("gpsimd", cc)
    S.barrier()


DEBUG_TAPS = ()


def build_fused():
    nc = bass.Bass("TRN2", target_bir_lowering=False, debug=False)
    ki, ko = "ExternalInput", "ExternalOutput"
    E_ = lambda n, s, d: _dram(nc, n, s, d, ki)
    taps = []

    def I_(n, s, d):
        ap = _dram(nc, n, s, d, "Internal")
        if n in DEBUG_TAPS:
            taps.append((n, ap, s, d))
        return ap

    x_own = E_("x_own", [2048, D], F32)
    flags = E_("cc_flags", [P, 2], F32)
    identb = E_("cc_ident", [P, P], BF16)
    es = ExitStack()
    with es, nc.Block() as block:
        S = Shared(nc, block, es)

        io_a = {"a_x": x_own, "cc_ident": identb,
                "a_xn_fm": I_("xn_loc", [D, 2048], BF16)}
        emit_stage_a(S, io_a, T=2048)

        xn_g = I_("xn_g", [2 * D, 2048], BF16)
        emit_allgather(S, xn_g, io_a["a_xn_fm"])
        b_xn = I_("b_xn", [D, T], BF16)
        emit_copy(S, b_xn[:, 0:2048], xn_g[0:D, :], D)
        emit_copy(S, b_xn[:, 2048:4096], xn_g[D:2 * D, :], D)

        io_b = {"b_xn_fm": b_xn,
                "cc_mask": E_("cc_mask", [P, 2048], BF16),
                "cc_cols_bf": E_("cc_cols_bf", [P, 4], BF16),
                "cc_row_f32": E_("cc_row_f32", [1, 768], F32),
                "b_y": I_("y_loc", [512, T], BF16)}
        for n in "qkv":
            io_b[f"b_w{n}"] = E_(f"b_w{n}", [D, 512], F32)
            io_b[f"b_b{n}"] = E_(f"b_b{n}", [1, 512], F32)
        emit_stage_b(S, io_b, T=T)

        y_g = I_("y_g", [1024, T], BF16)
        emit_allgather(S, y_g, io_b["b_y"])
        d_z = I_("d_z", [D, 2048], BF16)
        emit_select(S, d_z, y_g[:, 0:2048], y_g[:, 2048:4096], 1024, 2048, BF16, flags)

        BUF = CHALO + 2048
        xin_prev_fm_bf = io_a["a_xn_fm"]
        cf_out = None
        for lyr in range(2):
            pref = f"c{lyr}_"
            c_xin = I_(pref + "xin", [D, BUF], F32)
            if lyr == 0:
                emit_select(S, c_xin[:, 0:CHALO], None, xn_g[0:D, 512:2048],
                            D, CHALO, F32, flags, only_b=True)
                emit_copy(S, c_xin[:, CHALO:BUF], io_a["a_xn_fm"], D,
                          cast_via_sbuf=F32)
            else:
                ha_in = I_(pref + "ha_in", [D, CHALO], F32)
                emit_copy(S, ha_in, cf_out[:, 512:2048], D)
                ha_g = I_(pref + "ha_g", [2 * D, CHALO], F32)
                emit_allgather(S, ha_g, ha_in)
                emit_select(S, c_xin[:, 0:CHALO], None, ha_g[0:D, :],
                            D, CHALO, F32, flags, only_b=True)
                emit_copy(S, c_xin[:, CHALO:BUF], cf_out, D)
            io_c = {"c_xin_fm": c_xin,
                    "c_wqkv": E_(pref + "c_wqkv", [D, 1536], F32),
                    "c_wo": E_(pref + "c_wo", [512, D], F32),
                    "c_qs": E_(pref + "c_qs", [P, 1], F32),
                    "c_ks": E_(pref + "c_ks", [P, 1], F32),
                    "c_w0col": E_("c_w0col", [P, 2], F32),
                    "cc_addmask": E_("cc_addmask", [P, 256], F32),
                    "cc_rot": E_("cc_rot", [P, P], BF16),
                    "cc_cols_bf": io_b["cc_cols_bf"],
                    "cc_row_f32": io_b["cc_row_f32"],
                    "cc_ident": identb,
                    "c_x1": I_(pref + "x1", [D, 512 + 2048], F32)}
            for n in ("rq1", "rq2", "rk1", "rk2"):
                io_c[f"c_{n}"] = E_(f"c_{n}", [P, BUF], F32)
            emit_stage_c_attn(S, io_c, nwin_own=4)
            io_cf = {"cf_x1": io_c["c_x1"],
                     "cf_g": E_(pref + "cf_g", [P, 8], F32),
                     "cf_b": E_(pref + "cf_b", [P, 8], F32),
                     "cf_w1": E_(pref + "cf_w1", [D, 5460], F32),
                     "cf_b1": E_(pref + "cf_b1", [1, 5460], F32),
                     "cf_w2": E_(pref + "cf_w2", [2730, D], F32),
                     "cf_b2": E_(pref + "cf_b2", [P, 8], F32),
                     "cc_cols_f32": E_("cc_cols_f32", [P, 4], F32),
                     "cc_row_f32": io_b["cc_row_f32"],
                     "cf_out": I_(pref + "out", [D, 2048], F32)}
            emit_stage_c_ffn(S, io_cf)
            cf_out = io_cf["cf_out"]

        io_d = {"d_x": x_own, "d_z": d_z, "d_ffn": cf_out,
                "d_wproj": E_("d_wproj", [D, D], F32),
                "d_bprojrow": E_("d_bprojrow", [1, D], F32),
                "cc_ident_f32": E_("cc_ident_f32", [P, P], F32),
                "cc_row_f32": io_b["cc_row_f32"],
                "d_x1": I_("d_x1", [2048, D], F32),
                "d_xn1": I_("d_xn1", [2048, D], BF16)}
        emit_stage_d(S, io_d)

        xn1_g = I_("xn1_g", [T, D], BF16)
        emit_allgather(S, xn1_g, io_d["d_xn1"])
        e_xnfm = I_("e_xnfm", [D, T], BF16)
        emit_tm2fm(S, e_xnfm, xn1_g, T, D, identb)
        e_v = I_("e_v", [T, 512], BF16)
        emit_select(S, e_v, xn1_g[:, 0:512], xn1_g[:, 512:1024], T, 512, BF16, flags)
        e_vfm = I_("e_vfm", [512, T], BF16)
        emit_tm2fm(S, e_vfm, e_v, T, 512, identb)

        io_e = {"e_xnfm": e_xnfm, "e_v": e_v, "e_vfm": e_vfm,
                "cc_mask": io_c["cc_addmask"] if False else E_("cc_mask", [P, 2048], BF16),
                "cc_maskS": E_("cc_maskS", [P, 2048], BF16),
                "cc_cols_bf": io_b["cc_cols_bf"],
                "cc_row_f32": io_b["cc_row_f32"],
                "cc_ident": identb,
                "e_z": I_("z_loc", [512, T], BF16)}
        for n in ("q", "k", "k2"):
            io_e[f"e_w{n}"] = E_(f"e_w{n}", [D, 512], F32)
        emit_stage_e(S, io_e, T=T)

        z_g = I_("z_g", [1024, T], BF16)
        emit_allgather(S, z_g, io_e["e_z"])
        f_z = I_("f_z", [D, 2048], BF16)
        emit_select(S, f_z, z_g[:, 0:2048], z_g[:, 2048:4096], 1024, 2048, BF16, flags)

        io_f = {"f_x1": io_d["d_x1"], "f_z": f_z,
                "f_wproj": E_("f_wproj", [D, D], F32),
                "f_xn1": io_d["d_xn1"],
                "f_u1": E_("f_u1", [D, 256], F32),
                "f_v1": E_("f_v1", [256, 4096], F32),
                "f_b1row": E_("f_b1row", [1, 4096], F32),
                "f_u2": E_("f_u2", [4096, 256], F32),
                "f_v2": E_("f_v2", [256, D], F32),
                "f_b2row": E_("f_b2row", [1, D], F32),
                "cc_ident": identb,
                "cc_row_f32": io_b["cc_row_f32"],
                "f_out": _dram(nc, "f_out", [2048, D], F32, ko)}
        emit_stage_f(S, io_f)
        for n, ap, s, d in taps:
            dbg = _dram(nc, "dbg_" + n, s, d, ko)
            emit_copy(S, dbg, ap, s[0])
    return nc


def fused_in_maps(inputs):
    f32 = np.float32
    x0 = np.ascontiguousarray(inputs["L0_attn_x"], dtype=f32)
    CONST = {**np_consts(), **attn_consts(), **c_consts(), **e_consts()}
    CONST["cc_ident_f32"] = np.eye(P, dtype=f32)
    tabs = {t0: c_rotary_tables(t0)(CHALO + 2048) for t0 in (0, 2048)}
    wA = inputs["L0_attn_w_attn"].astype(f32)
    bA = inputs["L0_attn_b_attn"].astype(f32)
    wA1 = inputs["L1_attn_W_attn"].astype(f32)
    wK2 = inputs["L1_attn_W_k2"].astype(f32)
    maps = []
    for c in range(8):
        b, h = c // 2, c % 2
        t0 = 2048 * h
        fl = np.zeros((P, 2), f32)
        fl[:, h] = 1.0
        w0col = np.zeros((P, 2), f32)
        for wi in range(2):
            if t0 - 1024 + 512 * wi < 0:
                w0col[:, wi] = NEG_BIG
        m = {
            "x_own": np.ascontiguousarray(x0[b, 2048 * h:2048 * (h + 1)]),
            "cc_flags": fl,
            "c_w0col": w0col,
            **{k: CONST[k] for k in ("cc_ident", "cc_mask", "cc_cols_bf", "cc_row_f32",
                                     "cc_addmask", "cc_rot", "cc_cols_f32",
                                     "cc_ident_f32", "cc_maskS")},
            **{f"c_{n}": tabs[t0][n] for n in ("rq1", "rq2", "rk1", "rk2")},
            "d_wproj": inputs["L0_attn_w_proj"].astype(f32),
            "d_bprojrow": inputs["L0_attn_b_proj"].astype(f32).reshape(1, -1),
            "f_wproj": inputs["L1_attn_W_proj"].astype(f32),
            "f_u1": inputs["L1_ffn_U1"].astype(f32),
            "f_v1": inputs["L1_ffn_V1"].astype(f32),
            "f_b1row": inputs["L1_ffn_b1"].astype(f32).reshape(1, -1),
            "f_u2": inputs["L1_ffn_U2"].astype(f32),
            "f_v2": inputs["L1_ffn_V2"].astype(f32),
            "f_b2row": inputs["L1_ffn_b2"].astype(f32).reshape(1, -1),
        }
        for n, sl in (("q", 0), ("k", 1024), ("v", 2048)):
            m[f"b_w{n}"] = np.ascontiguousarray(wA[:, sl + 512 * h:sl + 512 * h + 512])
            m[f"b_b{n}"] = np.ascontiguousarray(bA[sl + 512 * h:sl + 512 * h + 512]).reshape(1, 512)
        m["e_wq"] = np.ascontiguousarray(wA1[:, 512 * h:512 * h + 512])
        m["e_wk"] = np.ascontiguousarray(wA1[:, 1024 + 512 * h:1024 + 512 * h + 512])
        m["e_wk2"] = np.ascontiguousarray(wK2[:, 512 * h:512 * h + 512])
        for lyr in range(2):
            pref = f"c{lyr}_"
            qs = inputs["L0_ffn_q_scale"][lyr].astype(f32)
            ks = inputs["L0_ffn_k_scale"][lyr].astype(f32)
            m[pref + "c_wqkv"] = inputs["L0_ffn_wqkv"][lyr].astype(f32)
            m[pref + "c_wo"] = inputs["L0_ffn_wo"][lyr].astype(f32)
            m[pref + "c_qs"] = np.concatenate([qs, qs]).reshape(P, 1)
            m[pref + "c_ks"] = np.concatenate([ks, ks]).reshape(P, 1)
            m[pref + "cf_g"] = np.ascontiguousarray(inputs["L0_ffn_ln_g"][lyr].astype(f32).reshape(8, P).T)
            m[pref + "cf_b"] = np.ascontiguousarray(inputs["L0_ffn_ln_b"][lyr].astype(f32).reshape(8, P).T)
            m[pref + "cf_w1"] = inputs["L0_ffn_w1"][lyr].astype(f32)
            m[pref + "cf_b1"] = inputs["L0_ffn_b1"][lyr].astype(f32).reshape(1, -1)
            m[pref + "cf_w2"] = inputs["L0_ffn_w2"][lyr].astype(f32)
            m[pref + "cf_b2"] = np.ascontiguousarray(inputs["L0_ffn_b2"][lyr].astype(f32).reshape(8, P).T)
        maps.append(m)
    return maps


def kernel_fused(**inputs):
    nc = _build("fused", build_fused)
    maps = fused_in_maps(inputs)
    res = run_bass_kernel_spmd(nc, maps, core_ids=list(range(8)))
    out = np.empty((4, 4096, 1024), np.float32)
    for c in range(8):
        b, h = c // 2, c % 2
        out[b, 2048 * h:2048 * (h + 1), :] = res.results[c]["f_out"]
    return out


FUSED = True


_NC_CACHE = {}


def _build(name, fn, *a, **kw):
    key = (name,) + tuple(a) + tuple(sorted(kw.items()))
    if key not in _NC_CACHE:
        _NC_CACHE[key] = fn(*a, **kw)
    return _NC_CACHE[key]


def kernel(**inputs):
    if FUSED:
        return kernel_fused(**inputs)
    f32 = np.float32
    x0 = np.ascontiguousarray(inputs["L0_attn_x"], dtype=f32)
    CONST = {**np_consts(), **attn_consts(), **c_consts(), **e_consts()}
    CONST["cc_ident_f32"] = np.eye(P, dtype=f32)
    cores = list(range(8))
    t0s = [2048 * (c % 2) for c in cores]
    bs = [c // 2 for c in cores]

    def run(nc, maps):
        return run_bass_kernel_spmd(nc, maps, core_ids=cores).results

    nc = _build("a", build_stage_a)
    ra = run(nc, [{"a_x": x0[bs[c]], "cc_ident": CONST["cc_ident"]} for c in cores])
    xnfm = [ra[c]["a_xn_fm"] for c in cores]

    wA = inputs["L0_attn_w_attn"].astype(f32)
    bA = inputs["L0_attn_b_attn"].astype(f32)
    nc = _build("b", build_stage_b)
    maps = []
    for c in cores:
        h = c % 2
        maps.append({
            "b_xn_fm": xnfm[c],
            "b_wq": np.ascontiguousarray(wA[:, 512 * h:512 * h + 512]),
            "b_wk": np.ascontiguousarray(wA[:, 1024 + 512 * h:1024 + 512 * h + 512]),
            "b_wv": np.ascontiguousarray(wA[:, 2048 + 512 * h:2048 + 512 * h + 512]),
            "b_bq": np.ascontiguousarray(bA[512 * h:512 * h + 512]).reshape(1, 512),
            "b_bk": np.ascontiguousarray(bA[1024 + 512 * h:1024 + 512 * h + 512]).reshape(1, 512),
            "b_bv": np.ascontiguousarray(bA[2048 + 512 * h:2048 + 512 * h + 512]).reshape(1, 512),
            "cc_mask": CONST["cc_mask"], "cc_cols_bf": CONST["cc_cols_bf"],
            "cc_row_f32": CONST["cc_row_f32"]})
    rb = run(nc, maps)
    ys = [rb[c]["b_y"] for c in cores]

    xin_full = [np.ascontiguousarray(xnfm[c].astype(f32)) for c in cores]
    tabs = {t0: c_rotary_tables(t0)(CHALO + 2048) for t0 in (0, 2048)}
    for lyr in range(2):
        nc = _build("c_attn", build_stage_c_attn)
        maps = []
        for c in cores:
            t0 = t0s[c]
            buf = np.zeros((D, CHALO + 2048), f32)
            lo = t0 - CHALO
            s0 = max(0, lo)
            buf[:, s0 - lo:] = xin_full[c][:, s0:t0 + 2048]
            w0col = np.zeros((P, 2), f32)
            for wi in range(2):
                if t0 - 1024 + 512 * wi < 0:
                    w0col[:, wi] = NEG_BIG
            qs = inputs["L0_ffn_q_scale"][lyr].astype(f32)
            ks = inputs["L0_ffn_k_scale"][lyr].astype(f32)
            maps.append({
                "c_xin_fm": buf,
                "c_wqkv": inputs["L0_ffn_wqkv"][lyr].astype(f32),
                "c_wo": inputs["L0_ffn_wo"][lyr].astype(f32),
                "c_qs": np.concatenate([qs, qs]).reshape(P, 1),
                "c_ks": np.concatenate([ks, ks]).reshape(P, 1),
                **{f"c_{n}": tabs[t0][n] for n in ("rq1", "rq2", "rk1", "rk2")},
                "c_w0col": w0col,
                "cc_addmask": CONST["cc_addmask"], "cc_rot": CONST["cc_rot"],
                "cc_cols_bf": CONST["cc_cols_bf"], "cc_row_f32": CONST["cc_row_f32"],
                "cc_ident": CONST["cc_ident"]})
        rc_ = run(nc, maps)
        x1s = [rc_[c]["c_x1"] for c in cores]

        nc = _build("c_ffn", build_stage_c_ffn)
        maps = []
        for c in cores:
            maps.append({
                "cf_x1": x1s[c],
                "cf_g": np.ascontiguousarray(inputs["L0_ffn_ln_g"][lyr].astype(f32).reshape(8, P).T),
                "cf_b": np.ascontiguousarray(inputs["L0_ffn_ln_b"][lyr].astype(f32).reshape(8, P).T),
                "cf_w1": inputs["L0_ffn_w1"][lyr].astype(f32),
                "cf_b1": inputs["L0_ffn_b1"][lyr].astype(f32).reshape(1, -1),
                "cf_w2": inputs["L0_ffn_w2"][lyr].astype(f32),
                "cf_b2": np.ascontiguousarray(inputs["L0_ffn_b2"][lyr].astype(f32).reshape(8, P).T),
                "cc_cols_f32": CONST["cc_cols_f32"], "cc_row_f32": CONST["cc_row_f32"]})
        rcf = run(nc, maps)
        outs = [rcf[c]["cf_out"] for c in cores]
        xin_full = [np.concatenate([outs[2 * bs[c]], outs[2 * bs[c] + 1]], axis=1)
                    for c in cores]
    ffn_out = outs

    nc = _build("d", build_stage_d)
    maps = []
    for c in cores:
        b, h = bs[c], c % 2
        z = np.concatenate([ys[2 * b][:, 2048 * h:2048 * (h + 1)],
                            ys[2 * b + 1][:, 2048 * h:2048 * (h + 1)]], axis=0)
        maps.append({
            "d_x": np.ascontiguousarray(x0[b][2048 * h:2048 * (h + 1)]),
            "d_z": np.ascontiguousarray(z),
            "d_ffn": ffn_out[c],
            "d_wproj": inputs["L0_attn_w_proj"].astype(f32),
            "d_bprojrow": inputs["L0_attn_b_proj"].astype(f32).reshape(1, -1),
            "cc_ident_f32": CONST["cc_ident_f32"], "cc_row_f32": CONST["cc_row_f32"]})
    rd = run(nc, maps)
    x1_tm = [rd[c]["d_x1"] for c in cores]
    xn1_tm = [rd[c]["d_xn1"] for c in cores]

    wA1 = inputs["L1_attn_W_attn"].astype(f32)
    wK2 = inputs["L1_attn_W_k2"].astype(f32)
    nc = _build("e", build_stage_e)
    maps = []
    xn1_full = [np.concatenate([xn1_tm[2 * bs[c]], xn1_tm[2 * bs[c] + 1]], axis=0)
                for c in cores]
    for c in cores:
        h = c % 2
        vsl = np.ascontiguousarray(xn1_full[c][:, 512 * h:512 * h + 512])
        maps.append({
            "e_xnfm": np.ascontiguousarray(xn1_full[c].T),
            "e_v": vsl, "e_vfm": np.ascontiguousarray(vsl.T),
            "e_wq": np.ascontiguousarray(wA1[:, 512 * h:512 * h + 512]),
            "e_wk": np.ascontiguousarray(wA1[:, 1024 + 512 * h:1024 + 512 * h + 512]),
            "e_wk2": np.ascontiguousarray(wK2[:, 512 * h:512 * h + 512]),
            "cc_mask": CONST["cc_mask"], "cc_maskS": CONST["cc_maskS"],
            "cc_cols_bf": CONST["cc_cols_bf"], "cc_row_f32": CONST["cc_row_f32"],
            "cc_ident": CONST["cc_ident"]})
    re_ = run(nc, maps)
    zs = [re_[c]["e_z"] for c in cores]

    nc = _build("f", build_stage_f)
    maps = []
    for c in cores:
        b, h = bs[c], c % 2
        z = np.concatenate([zs[2 * b][:, 2048 * h:2048 * (h + 1)],
                            zs[2 * b + 1][:, 2048 * h:2048 * (h + 1)]], axis=0)
        maps.append({
            "f_x1": x1_tm[c], "f_z": np.ascontiguousarray(z),
            "f_wproj": inputs["L1_attn_W_proj"].astype(f32),
            "f_xn1": xn1_tm[c],
            "f_u1": inputs["L1_ffn_U1"].astype(f32),
            "f_v1": inputs["L1_ffn_V1"].astype(f32),
            "f_b1row": inputs["L1_ffn_b1"].astype(f32).reshape(1, -1),
            "f_u2": inputs["L1_ffn_U2"].astype(f32),
            "f_v2": inputs["L1_ffn_V2"].astype(f32),
            "f_b2row": inputs["L1_ffn_b2"].astype(f32).reshape(1, -1),
            "cc_ident": CONST["cc_ident"], "cc_row_f32": CONST["cc_row_f32"]})
    rf = run(nc, maps)
    out = np.empty((4, 4096, 1024), np.float32)
    for c in cores:
        b, h = bs[c], c % 2
        out[b, 2048 * h:2048 * (h + 1), :] = rf[c]["f_out"]
    return out
